# version 05
# baseline (speedup 1.0000x reference)
import numpy as np
import concourse.bass as bass
import concourse.mybir as mybir
from concourse.ap import AP as APc
from concourse.bass_utils import run_bass_kernel_spmd

F32 = mybir.dt.float32
BF16 = mybir.dt.bfloat16
AF = mybir.ActivationFunctionType
AL = mybir.AluOpType
AX = mybir.AxisListType

ENGINES = ("tensor", "vector", "scalar", "gpsimd", "sync")
QUEUES = ("sync", "gpsimd", "scalar")
NRING = 4

B_LOC = 2
T = 2048
D = 1024
NTOK = B_LOC * T


class Prog:

    def __init__(self, nc):
        self.nc = nc
        self.streams = {e: [] for e in ENGINES}
        self.semh = {}
        self.semc = {}
        self.qidx = {q: 0 for q in QUEUES}
        self.sb = {}
        self.ps = {}
        self.dram = {}

    def sem(self, name):
        if name not in self.semh:
            self.semh[name] = self.nc.alloc_semaphore(name)
            self.semc[name] = 0
        return self.semh[name]

    def inc(self, name, by=1):
        self.sem(name)
        self.semc[name] += by
        return self.semc[name]

    def cnt(self, name):
        self.sem(name)
        return self.semc[name]

    def sbuf(self, name, shape, dtype):
        if name not in self.sb:
            self.sb[name] = self.nc.alloc_sbuf_tensor(name, shape, dtype)
        return self.sb[name]

    ARENA_BYTES = 192 * 1024

    def arena_reset(self):
        self._arena_off = 0

    def abuf(self, name, shape, dtype):
        ar = self.sbuf("ARENA", [128, self.ARENA_BYTES], mybir.dt.uint8)
        if not hasattr(self, "_arena_off"):
            self._arena_off = 0
        import math
        nel = 1
        for s in shape[1:]:
            nel *= s
        nb = nel * mybir.dt.size(dtype)
        off = self._arena_off
        self._arena_off = (off + nb + 511) // 512 * 512
        assert self._arena_off <= self.ARENA_BYTES, f"arena overflow at {name}: {self._arena_off}"
        v = ar.ap()[:, off:off + nb].bitcast(dtype)
        if len(shape) > 2:
            dims = " ".join(f"d{i}" for i in range(1, len(shape)))
            kw = {f"d{i}": shape[i] for i in range(1, len(shape))}
            v = v.rearrange(f"p ({dims}) -> p {dims}", **kw)
        if shape[0] != 128:
            v = v[0:shape[0]]
        return v

    def psum(self, name, shape, dtype=F32):
        if name not in self.ps:
            self.ps[name] = self.nc.alloc_psum_tensor(name, shape, dtype)
        return self.ps[name]

    def dram_scratch(self, name, shape, dtype):
        if name not in self.dram:
            self.dram[name] = self.nc.dram_tensor(name, shape, dtype, kind="Internal")
        return self.dram[name].ap()

    def emit(self, eng, fn):
        self.streams[eng].append(fn)

    def wait(self, eng, sem_name, val):
        if val <= 0:
            return
        s = self.sem(sem_name)
        self.emit(eng, lambda E, s=s, v=val: E.wait_ge(s, v))

    def wait_tok(self, eng, tok):
        if tok is None:
            return
        self.wait(eng, tok[0], tok[1])

    def wait_toks(self, eng, toks):
        best = {}
        for t in toks:
            if t is None:
                continue
            best[t[0]] = max(best.get(t[0], 0), t[1])
        for s, v in best.items():
            self.wait(eng, s, v)

    def wait_queue(self, eng, queue):
        for j in range(NRING):
            name = f"{queue}q{j}"
            if name in self.semc and self.semc[name] > 0:
                self.wait(eng, name, self.semc[name])

    def dma(self, queue, out_ap, in_ap, transpose=False, wait=None, allow_slow=False):
        j = self.qidx[queue] % NRING
        self.qidx[queue] += 1
        name = f"{queue}q{j}"
        prev = self.cnt(name)
        tgt = self.inc(name, 16)
        s = self.semh[name]
        if prev > 0:
            self.emit(queue, lambda E, s=s, v=prev: E.wait_ge(s, v))
        if wait is not None:
            self.wait_tok(queue, wait)

        def go(E, o=out_ap, i=in_ap, s=s, tr=transpose, asl=allow_slow):
            if tr:
                ins = E.dma_start(out=o, in_=i, transpose=True)
            elif asl:
                ins = E.dma_start(out=o, in_=i, allow_slow_non_contiguous=True)
            else:
                ins = E.dma_start(out=o, in_=i)
            return ins.then_inc(s, 16)
        self.emit(queue, go)
        return (name, tgt)

    ENGINE_SEMS = {"vector": ("ev",), "scalar": ("ac", "ax"), "gpsimd": ("po",), "tensor": ("mm",)}

    def barrier(self, engines=ENGINES):
        for q, eng in (("sync", "sync"), ("gpsimd", "gpsimd"), ("scalar", "scalar")):
            self.wait_queue(eng, q)
        for eng, sems in self.ENGINE_SEMS.items():
            for sname in sems:
                if sname in self.semc and self.semc[sname] > 0:
                    self.wait(eng, sname, self.semc[sname])
        self.sem("PB")
        for e in engines:
            if e in ("vector", "scalar", "gpsimd", "tensor"):
                self.emit(e, lambda E: E.drain().then_inc(self.semh["PB"], 1))
            else:
                self.emit(e, lambda E: E.nop().then_inc(self.semh["PB"], 1))
        self.semc["PB"] += len(engines)
        tgt = self.semc["PB"]
        for e in engines:
            self.emit(e, lambda E, t=tgt: E.wait_ge(self.semh["PB"], t))

    def build(self):
        with self.nc.Block() as block:
            block.tensor(lambda E: [f(E) for f in self.streams["tensor"]])
            block.vector(lambda E: [f(E) for f in self.streams["vector"]])
            block.scalar(lambda E: [f(E) for f in self.streams["scalar"]])
            block.gpsimd(lambda E: [f(E) for f in self.streams["gpsimd"]])
            block.sync(lambda E: [f(E) for f in self.streams["sync"]])
        return self.nc



def load_x_fm(p, x_sb, x_dram_tm, ntok, K, queue="sync", colbase=0):
    nk = K // 128
    toks = []
    for k in range(nk):
        toks.append(p.dma(queue, x_sb[:, k, 0:ntok],
                          x_dram_tm[:, colbase + k * 128: colbase + (k + 1) * 128],
                          transpose=True))
    return toks


def load_w_cast(p, w_sb_ap, w_dram):
    if w_dram.shape[0] > 128:
        src = w_dram.rearrange("(k p) n -> p k n", p=128)
    else:
        src = w_dram
    return p.dma("gpsimd", w_sb_ap, src)


def bcast_row(p, dst_sb, row_dram_1d, queue="sync"):
    return p.dma(queue, dst_sb, row_dram_1d.rearrange("(o n) -> o n", o=1).partition_broadcast(128))


def _kchunks(K):
    ch = [(i * 128, 128) for i in range(K // 128)]
    if K % 128:
        ch.append((K - K % 128, K % 128))
    return ch



def emit_ln(p, uid, x_dram_f32, out_tm_bf16, ntok=NTOK, d=D,
            gamma_dram=None, beta_dram=None, eps=1e-5,
            out_tm_f32=None, double_ln_eps=None, rms=False):
    G = 4
    ntile = ntok // 128
    ngrp = ntile // G
    p.arena_reset()
    xs = p.abuf("ln_x", [128, G, d], F32)
    os16 = p.abuf("ln_o16", [128, G, d], BF16)
    os32 = p.abuf("ln_o32", [128, G, d], F32) if (out_tm_f32 is not None or gamma_dram is not None) else None
    sq = p.abuf("ln_sq", [128, 4, d], F32)
    st = p.abuf("ln_st", [128, G, 16], F32)
    gb = None
    if gamma_dram is not None:
        gb = p.abuf("ln_gb", [128, 2, d], F32)
        g1 = bcast_row(p, gb[:, 0], gamma_dram)
        p.wait_toks("vector", [g1])
        if beta_dram is not None:
            g2 = bcast_row(p, gb[:, 1], beta_dram)
            p.wait_toks("gpsimd", [g2])
    x_r = x_dram_f32.rearrange("(g a p) n -> g p a n", p=128, a=G)
    o16_r = out_tm_bf16.rearrange("(g a p) n -> g p a n", p=128, a=G)
    o32_r = out_tm_f32.rearrange("(g a p) n -> g p a n", p=128, a=G) if out_tm_f32 is not None else None
    inv_d = 1.0 / d
    last_stores = []

    for g in range(ngrp):
        p.wait_toks("sync", last_stores)
        ld_t = p.dma("sync", xs[:], x_r[g])
        p.wait_tok("vector", ld_t)
        for a in range(G):
            p.inc("ev")
            p.emit("vector", lambda E, a=a: E.reduce_sum(st[:, a, 0:1], xs[:, a], axis=AX.X).then_inc(p.semh["ev"], 1))
        ev1 = p.cnt("ev")
        p.wait_tok("scalar", ld_t)
        for a in range(G):
            p.inc("ac")
            p.emit("scalar", lambda E, a=a: E.activation(sq[:, a], xs[:, a], AF.Square,
                   accum_out=st[:, a, 1:2]).then_inc(p.semh["ac"], 1))
        ac1 = p.cnt("ac")
        p.wait("vector", "ev", ev1)
        p.wait("vector", "ac", ac1)
        if rms:
            for a in range(G):
                p.inc("ev")
                p.emit("vector", lambda E, a=a: E.tensor_scalar(
                    st[:, a, 5:6], st[:, a, 1:2], inv_d, float(eps), AL.mult, AL.add).then_inc(p.semh["ev"], 1))
            p.emit("vector", lambda E: E.memset(st[:, :, 4], 0.0))
        else:
            p.emit("vector", lambda E: E.tensor_scalar(st[:, :, 2], st[:, :, 0], inv_d, None, AL.mult))
            p.inc("ev", 2)
            p.emit("vector", lambda E: E.tensor_scalar(st[:, :, 3], st[:, :, 1], inv_d, float(eps), AL.mult, AL.add).then_inc(p.semh["ev"], 1))
            p.emit("vector", lambda E: E.tensor_scalar(st[:, :, 4], st[:, :, 0], -inv_d, None, AL.mult).then_inc(p.semh["ev"], 1))
            ev2 = p.cnt("ev")
            p.wait("vector", "ev", ev2)
            for a in range(G):
                p.inc("ev")
                p.emit("vector", lambda E, a=a: E.scalar_tensor_tensor(
                    st[:, a, 5:6], st[:, a, 2:3], st[:, a, 4:5], st[:, a, 3:4], AL.mult, AL.add).then_inc(p.semh["ev"], 1))
        ev3 = p.cnt("ev")
        p.wait("scalar", "ev", ev3)
        ac2 = p.inc("ac")
        p.emit("scalar", lambda E: E.activation(st[:, :, 6], st[:, :, 5], AF.Sqrt).then_inc(p.semh["ac"], 1))
        p.wait("vector", "ac", ac2)
        ev4 = p.inc("ev")
        p.emit("vector", lambda E: E.reciprocal(st[:, :, 8], st[:, :, 6]).then_inc(p.semh["ev"], 1))
        p.wait("vector", "ev", ev4)
        if double_ln_eps is not None:
            for a in range(G):
                p.inc("ev")
                p.emit("vector", lambda E, a=a: E.tensor_scalar(
                    st[:, a, 10:11], st[:, a, 8:9], st[:, a, 8:9], -float(eps), AL.mult, AL.mult).then_inc(p.semh["ev"], 1))
            p.wait("vector", "ev", p.cnt("ev"))
            ev5 = p.inc("ev")
            p.emit("vector", lambda E: E.tensor_scalar(st[:, :, 11], st[:, :, 10], 1.0 + float(double_ln_eps), None, AL.add).then_inc(p.semh["ev"], 1))
            p.wait("scalar", "ev", ev5)
            ac3 = p.inc("ac")
            p.emit("scalar", lambda E: E.activation(st[:, :, 12], st[:, :, 11], AF.Sqrt).then_inc(p.semh["ac"], 1))
            p.wait("vector", "ac", ac3)
            ev6 = p.inc("ev")
            p.emit("vector", lambda E: E.reciprocal(st[:, :, 13], st[:, :, 12]).then_inc(p.semh["ev"], 1))
            p.wait("vector", "ev", ev6)
            ev6b = p.inc("ev")
            p.emit("vector", lambda E: E.tensor_mul(st[:, :, 8], st[:, :, 8], st[:, :, 13]).then_inc(p.semh["ev"], 1))
            p.wait("vector", "ev", ev6b)
        ev7 = p.inc("ev")
        p.emit("vector", lambda E: E.tensor_mul(st[:, :, 9], st[:, :, 4], st[:, :, 8]).then_inc(p.semh["ev"], 1))
        p.wait("scalar", "ev", ev7)
        if gamma_dram is None:
            for a in range(G):
                p.inc("ac")
                p.emit("scalar", lambda E, a=a: E.activation(os16[:, a], xs[:, a], AF.Identity,
                       bias=st[:, a, 9:10], scale=st[:, a, 8:9]).then_inc(p.semh["ac"], 1))
                if os32 is not None:
                    p.inc("ac")
                    p.emit("scalar", lambda E, a=a: E.activation(os32[:, a], xs[:, a], AF.Identity,
                           bias=st[:, a, 9:10], scale=st[:, a, 8:9]).then_inc(p.semh["ac"], 1))
            done = ("ac", p.cnt("ac"))
        else:
            for a in range(G):
                p.inc("ac")
                p.emit("scalar", lambda E, a=a: E.activation(sq[:, a], xs[:, a], AF.Identity,
                       bias=st[:, a, 9:10], scale=st[:, a, 8:9]).then_inc(p.semh["ac"], 1))
            ac4 = p.cnt("ac")
            p.wait("vector", "ac", ac4)
            for a in range(G):
                p.inc("ev")
                p.emit("vector", lambda E, a=a: E.tensor_mul(os32[:, a], sq[:, a], gb[:, 0]).then_inc(p.semh["ev"], 1))
            ev8 = p.cnt("ev")
            if beta_dram is not None:
                p.wait("gpsimd", "ev", ev8)
                for a in range(G):
                    p.inc("po")
                    p.emit("gpsimd", lambda E, a=a: E.tensor_add(os32[:, a], os32[:, a], gb[:, 1]).then_inc(p.semh["po"], 1))
                po1 = p.cnt("po")
                p.wait("scalar", "po", po1)
            else:
                p.wait("scalar", "ev", ev8)
            for a in range(G):
                p.inc("ac")
                p.emit("scalar", lambda E, a=a: E.activation(os16[:, a], os32[:, a], AF.Identity).then_inc(p.semh["ac"], 1))
            done = ("ac", p.cnt("ac"))
        p.wait("gpsimd", done[0], done[1])
        last_stores = [p.dma("gpsimd", o16_r[g], os16[:])]
        if o32_r is not None:
            last_stores.append(p.dma("gpsimd", o32_r[g], os32[:]))
    p.barrier()



def emit_mm_fm(p, uid, outT_dram, w_dram, x_src, K, Mout, ntok=NTOK,
               bias_dram=None, act=None, x_src_is_fm=False, extra_outT=None,
               skip_x_load=False, x_cast=False, out_dtype=BF16):
    kch = _kchunks(K)
    mch = _kchunks(Mout)
    nk, nm, ncol = len(kch), len(mch), ntok // 512
    p.arena_reset()
    xsb = p.abuf("mm_x", [128, 8, 4096], BF16)
    wsb = p.abuf("mm_w", [128, 8, 4096], BF16)
    osb = p.abuf("mm_o", [128, 2, 4096], out_dtype)
    ps = p.psum("ps_mm", [128, 2, 512], F32)
    assert nk <= 8 and ntok <= 4096 and Mout <= 4096
    bsb = None
    if bias_dram is not None:
        bsb = p.abuf(f"mm_bfm_{uid}", [128, max(nm, 1)], F32)
        if Mout >= 128:
            bt = p.dma("sync", bsb[:], bias_dram.rearrange("(m p) -> p m", p=128), allow_slow=True)
        else:
            bt = p.dma("sync", bsb[0:Mout, :], bias_dram.rearrange("(m p) -> p m", p=Mout), allow_slow=True)
        p.wait_tok("scalar", bt)
    wl_t = load_w_cast(p, wsb[0:kch[0][1], 0:nk, 0:Mout], w_dram)
    if skip_x_load:
        ld = []
    elif x_src_is_fm:
        q = "gpsimd" if x_cast else "sync"
        ld = [p.dma(q, xsb[0:kch[0][1], 0:nk, 0:ntok],
                    x_src.rearrange("(k p) t -> p k t", p=128) if x_src.shape[0] > 128 else x_src)]
    else:
        ld = load_x_fm(p, xsb, x_src, ntok, K)
    p.wait_tok("tensor", wl_t)
    p.wait_toks("tensor", ld)
    func = act if act is not None else AF.Identity
    ac0 = p.cnt("ac")
    store_toks = []
    gi = 0
    for m in range(nm):
        mp = mch[m][1]
        ob = m % 2
        for c in range(ncol):
            bank = gi % 2
            if gi >= 2:
                p.wait("tensor", "ac", ac0 + gi - 1)
            for k in range(nk):
                fin = (k == nk - 1)
                if fin:
                    p.inc("mm")
                kp = kch[k][1]
                p.emit("tensor", lambda E, o=ps[0:mp, bank], l=wsb[0:kp, k, m*128:m*128+mp],
                       r=xsb[0:kp, k, c*512:(c+1)*512], s0=(k == 0), s1=fin, f=fin:
                       E.matmul(o, l, r, start=s0, stop=s1).then_inc(p.semh["mm"], 1) if f
                       else E.matmul(o, l, r, start=s0, stop=s1))
            p.wait("scalar", "mm", p.cnt("mm"))
            if m >= 2 and c == 0:
                p.wait_toks("scalar", store_toks[m - 2])
            b = bsb[0:mp, m:m+1] if bsb is not None else 0.0
            p.inc("ac")
            p.emit("scalar", lambda E, i=ps[0:mp, bank], o=osb[0:mp, ob, c*512:(c+1)*512], b=b, f=func:
                   E.activation(o, i, f, bias=b, scale=1.0).then_inc(p.semh["ac"], 1))
            gi += 1
        p.wait("gpsimd", "ac", p.cnt("ac"))
        row_toks = [p.dma("gpsimd", outT_dram[m*128:m*128+mp, 0:ntok], osb[0:mp, ob, 0:ntok])]
        if extra_outT is not None:
            row_toks.append(p.dma("gpsimd", extra_outT[m*128:m*128+mp, 0:ntok], osb[0:mp, ob, 0:ntok]))
        store_toks.append(row_toks)
    p.barrier()


def emit_mm_tm(p, uid, out_dram, x_src, w_dram, K, Nout, ntok=NTOK,
               bias_dram=None, resid_dram=None, resid_out_f32=None,
               x_src_is_fm=False, post_scale=None):
    kch = _kchunks(K)
    nk, nt, ncol = len(kch), ntok // 128, Nout // 512
    p.arena_reset()
    xsb = p.abuf("mm_x", [128, 8, 4096], BF16)
    wsb = p.abuf("mm_w", [128, 8, 4096], BF16)
    ps = p.psum("ps_mm", [128, 2, 512], F32)
    assert nk <= 8 and Nout <= 1024
    osb = p.abuf("mm_to", [128, 2, Nout], BF16) if out_dram is not None else None
    osb32 = p.abuf("mm_to32", [128, 2, Nout], F32) if resid_out_f32 is not None else None
    rsb = p.abuf("mm_tr", [128, 2, Nout], F32) if resid_dram is not None else None
    bsb = None
    if bias_dram is not None:
        bsb = p.abuf(f"mm_btm_{uid}", [128, Nout], F32)
        bt = bcast_row(p, bsb[:], bias_dram)
        p.wait_tok("gpsimd" if resid_dram is not None else "vector", bt)
    gsb = accsb = None
    if post_scale is not None:
        ps_gates, ps_col, ps_ng, ps_acc = post_scale
        gsb = p.abuf("mm_gs", [128, 2, ps_ng], F32)
        accsb = p.abuf("mm_acc", [128, 2, Nout], F32)
    wl_t = load_w_cast(p, wsb[0:kch[0][1], 0:nk, 0:Nout], w_dram)
    if x_src_is_fm:
        ld = [p.dma("sync", xsb[0:kch[0][1], 0:nk, 0:ntok],
                    x_src.rearrange("(k p) t -> p k t", p=128) if x_src.shape[0] > 128 else x_src)]
    else:
        ld = load_x_fm(p, xsb, x_src, ntok, K)
    p.wait_tok("tensor", wl_t)
    p.wait_toks("tensor", ld)
    ev0 = p.cnt("ev")
    evs_per_tile = ncol * (2 if post_scale is not None else 1)
    store_toks = []
    for t in range(nt):
        tb = t % 2
        rld = None
        if rsb is not None:
            w = ("ev", ev0 + (t - 1) * evs_per_tile) if t >= 2 else None
            rld = p.dma("sync", rsb[:, tb], resid_dram[t*128:(t+1)*128, 0:Nout], wait=w)
        if rsb is not None and bsb is not None:
            p.wait_tok("gpsimd", rld)
            p.inc("po")
            p.emit("gpsimd", lambda E, tb=tb: E.tensor_add(rsb[:, tb, 0:Nout], rsb[:, tb, 0:Nout], bsb[:, 0:Nout]).then_inc(p.semh["po"], 1))
        if post_scale is not None:
            wp = ("ev", ev0 + (t - 1) * evs_per_tile) if t >= 2 else None
            gld = p.dma("sync", gsb[:, tb], ps_gates[t*128:(t+1)*128, 0:ps_ng], wait=wp)
            ald = p.dma("sync", accsb[:, tb], ps_acc[t*128:(t+1)*128, 0:Nout])
        for c in range(ncol):
            gi = t * ncol + c
            bank = gi % 2
            if gi >= 2:
                p.wait("tensor", "ev", ev0 + (gi - 1) * (evs_per_tile // ncol))
            for k in range(nk):
                fin = (k == nk - 1)
                if fin:
                    p.inc("mm")
                kp = kch[k][1]
                p.emit("tensor", lambda E, o=ps[:, bank], l=xsb[0:kp, k, t*128:(t+1)*128],
                       r=wsb[0:kp, k, c*512:(c+1)*512], s0=(k == 0), s1=fin, f=fin:
                       E.matmul(o, l, r, start=s0, stop=s1).then_inc(p.semh["mm"], 1) if f
                       else E.matmul(o, l, r, start=s0, stop=s1))
            p.wait("vector", "mm", p.cnt("mm"))
            if t >= 2 and c == 0:
                p.wait_toks("vector", store_toks[t - 2])
            sl = slice(c*512, (c+1)*512)
            if rsb is not None:
                if bsb is not None:
                    p.wait("vector", "po", p.cnt("po"))
                else:
                    p.wait_tok("vector", rld)
                p.inc("ev")
                p.emit("vector", lambda E, i=ps[:, bank], r=rsb[:, tb, sl], o=osb32[:, tb, sl]:
                       E.tensor_add(o, i, r).then_inc(p.semh["ev"], 1))
                if post_scale is not None:
                    p.wait_tok("vector", gld)
                    p.wait_tok("vector", ald)
                    p.wait("vector", "ev", p.cnt("ev"))
                    p.inc("ev")
                    p.emit("vector", lambda E, tb=tb, sl=sl:
                           E.scalar_tensor_tensor(osb32[:, tb, sl], osb32[:, tb, sl],
                                                  gsb[:, tb, ps_col:ps_col+1],
                                                  accsb[:, tb, sl], AL.mult, AL.add).then_inc(p.semh["ev"], 1))
                if osb is not None:
                    p.wait("scalar", "ev", p.cnt("ev"))
                    if t >= 2 and c == 0:
                        p.wait_toks("scalar", store_toks[t - 2])
                    p.inc("ac")
                    p.emit("scalar", lambda E, i=osb32[:, tb, sl], o=osb[:, tb, sl]:
                           E.activation(o, i, AF.Identity).then_inc(p.semh["ac"], 1))
            else:
                dst = osb[:, tb, sl] if osb is not None else osb32[:, tb, sl]
                if bsb is not None:
                    p.inc("ev")
                    p.emit("vector", lambda E, i=ps[:, bank], b=bsb[:, sl], o=dst:
                           E.tensor_add(o, i, b).then_inc(p.semh["ev"], 1))
                else:
                    p.inc("ev")
                    p.emit("vector", lambda E, i=ps[:, bank], o=dst:
                           E.tensor_scalar(o, i, 1.0, None, AL.mult).then_inc(p.semh["ev"], 1))
        tile_toks = []
        p.wait("gpsimd", "ev", p.cnt("ev"))
        if post_scale is not None:
            p.wait("gpsimd", "po", p.cnt("po"))
        if osb is not None:
            p.wait("gpsimd", "ac", p.cnt("ac"))
            tile_toks.append(p.dma("gpsimd", out_dram[t*128:(t+1)*128, 0:Nout], osb[:, tb, 0:Nout]))
        if osb32 is not None:
            tile_toks.append(p.dma("gpsimd", resid_out_f32[t*128:(t+1)*128, 0:Nout], osb32[:, tb, 0:Nout]))
        store_toks.append(tile_toks)
    p.barrier()



def emit_attn(p, uid, qT_dram, kT_dram, v_dram_tm, outT_dram, H, dq, dv,
              scale, ntok=NTOK, t_seq=T, causal=False, k_shared=False,
              v_cols_base=0, extra_score=None, mask_tiles=None, v_is_fm=False):
    nb = ntok // t_seq
    nqt = t_seq // 128
    nkc = t_seq // 512
    nc2 = t_seq // 128
    p.arena_reset()
    qsb = p.abuf("at_q", [128, t_seq], BF16)
    ksb = p.abuf("at_k", [128, t_seq], BF16)
    vsb = p.abuf("at_v", [128, t_seq // 128, 128], BF16)
    exs = p.abuf("at_ex", [128, 2, t_seq], F32)
    absb = p.abuf("at_ab", [128, 2, t_seq], BF16)
    attn_tsb = p.abuf("at_tT", [128, 2, 512], BF16)
    bdsb = p.abuf("at_bd", [128, 2, 512], BF16) if extra_score is not None else None
    stat = p.abuf("at_st", [128, 2, 8], F32)
    osb = p.abuf("at_o", [128, 2, 512], BF16)
    ps_sc = p.psum("ps_sc", [128, 4, 512], F32)
    ps_mm = p.psum("ps_mm", [128, 2, 512], F32)
    sc_flat = ps_sc.ap().rearrange("p a b -> p (a b)")
    atile = [p.dram_scratch(f"attn_scr{i}", [t_seq, t_seq], BF16) for i in range(2)]

    bh = 0
    mm_end_prev = 0
    ax0 = p.cnt("ax")
    po0 = p.cnt("po")
    ostore_toks = []
    for b in range(nb):
        kld = None
        for h in range(H):
            par = bh % 2
            qld = p.dma("sync", qsb[0:dq, 0:t_seq],
                        qT_dram[h*dq:(h+1)*dq, b*t_seq:(b+1)*t_seq],
                        wait=("mm", mm_end_prev))
            if (not k_shared) or h == 0:
                krows = kT_dram[0:dq, b*t_seq:(b+1)*t_seq] if k_shared else kT_dram[h*dq:(h+1)*dq, b*t_seq:(b+1)*t_seq]
                kld = p.dma("sync", ksb[0:dq, 0:t_seq], krows, wait=("mm", mm_end_prev))
            vtoks = []
            if v_is_fm:
                for c in range(t_seq // 128):
                    vtoks.append(p.dma("scalar", vsb[:, c, 0:dv],
                                 v_dram_tm[h*dv:(h+1)*dv, b*t_seq + c*128: b*t_seq + (c+1)*128],
                                 transpose=True, wait=("mm", mm_end_prev) if c < 4 else None))
            else:
                vtoks.append(p.dma("gpsimd",
                             v_dram_tm[b*t_seq:(b+1)*t_seq, v_cols_base + h*dv: v_cols_base + (h+1)*dv].rearrange("(c p) d -> p c d", p=128),
                             wait=("mm", mm_end_prev)) if False else
                             p.dma("gpsimd",
                             vsb[:, :, 0:dv],
                             v_dram_tm[b*t_seq:(b+1)*t_seq, v_cols_base + h*dv: v_cols_base + (h+1)*dv].rearrange("(c p) d -> p c d", p=128),
                             wait=("mm", mm_end_prev)))
            p.wait_tok("tensor", qld)
            p.wait_tok("tensor", kld)
            astore_toks = []
            for qt in range(nqt):
                used_kc = nkc if not causal else (qt // 4 + 1)
                used_k = used_kc * 512
                qpar = qt % 2
                p.wait("tensor", "ax", p.cnt("ax"))
                for kc in range(used_kc):
                    p.inc("mm")
                    p.emit("tensor", lambda E, o=ps_sc[:, kc], l=qsb[0:dq, qt*128:(qt+1)*128],
                           r=ksb[0:dq, kc*512:(kc+1)*512]:
                           E.matmul(o, l, r, start=True, stop=True).then_inc(p.semh["mm"], 1))
                mm_t = p.cnt("mm")
                if extra_score is not None:
                    for kc in range(used_kc):
                        src = extra_score(b, h, qt, kc)
                        if src is None:
                            continue
                        bt = p.dma("sync", bdsb[:, kc % 2], src, wait=("ev", max(0, p.cnt("ev") - 1)))
                        p.wait_tok("vector", bt)
                        p.wait("vector", "mm", mm_t)
                        p.inc("ev")
                        p.emit("vector", lambda E, o=ps_sc[:, kc], i=bdsb[:, kc % 2]:
                               E.tensor_add(o, o, i).then_inc(p.semh["ev"], 1))
                if causal:
                    dkc = used_kc - 1
                    p.wait("vector", "mm", mm_t)
                    p.wait("vector", "ev", p.cnt("ev"))
                    p.inc("ev")
                    p.emit("vector", lambda E, o=ps_sc[:, dkc, :], m=mask_tiles[:, qt % 4, :]:
                           E.tensor_add(o, o, m).then_inc(p.semh["ev"], 1))
                p.wait("vector", "mm", mm_t)
                p.wait("vector", "ev", p.cnt("ev"))
                p.inc("ev")
                p.emit("vector", lambda E, o=stat[:, qpar, 0:1], i=sc_flat[:, 0:used_k]:
                       E.reduce_max(o, i, axis=AX.X, negate=True).then_inc(p.semh["ev"], 1))
                p.wait("vector", "ev", p.cnt("ev"))
                p.inc("ev")
                p.emit("vector", lambda E, o=stat[:, qpar, 1:2], i=stat[:, qpar, 0:1], s=float(scale):
                       E.tensor_scalar(o, i, s, None, AL.mult).then_inc(p.semh["ev"], 1))
                p.wait("scalar", "ev", p.cnt("ev"))
                p.wait("scalar", "po", max(0, p.cnt("po") - 1))
                p.inc("ax")
                p.emit("scalar", lambda E, o=exs[:, qpar, 0:used_k], i=sc_flat[:, 0:used_k],
                       bias=stat[:, qpar, 1:2], s=float(scale), ao=stat[:, qpar, 2:3]:
                       E.activation(o, i, AF.Exp, bias=bias, scale=s, accum_out=ao).then_inc(p.semh["ax"], 1))
                p.wait("vector", "ax", p.cnt("ax"))
                p.inc("ev")
                p.emit("vector", lambda E, o=stat[:, qpar, 3:4], i=stat[:, qpar, 2:3]:
                       E.reciprocal(o, i).then_inc(p.semh["ev"], 1))
                p.wait("gpsimd", "ev", p.cnt("ev"))
                if len(astore_toks) >= 2:
                    p.wait_tok("gpsimd", astore_toks[-2])
                p.inc("po")
                p.emit("gpsimd", lambda E, o=absb[:, qpar, 0:used_k], i=exs[:, qpar, 0:used_k],
                       s=stat[:, qpar, 3:4]:
                       E.tensor_scalar(o, i, s, None, AL.mult).then_inc(p.semh["po"], 1))
                ast = p.dma("gpsimd", atile[par][qt*128:(qt+1)*128, 0:used_k],
                            absb[:, qpar, 0:used_k], wait=("po", p.cnt("po")))
                astore_toks.append(ast)
            p.wait_toks("tensor", vtoks)
            mmB = []
            for qc in range(nkc):
                used_c = nc2 if not causal else 4 * (qc + 1)
                cpar = qc % 2
                p.wait("tensor", "ac", p.cnt("ac") if qc < 2 else p.cnt("ac") - 1)
                for c2 in range(used_c):
                    w = None
                    if qc == 0 and c2 == 0:
                        p.wait_toks("sync", astore_toks[-4:])
                    elif len(mmB) >= 2:
                        w = ("mm", mmB[-2])
                    at = p.dma("sync", attn_tsb[:, c2 % 2, :],
                               atile[par][qc*512:(qc+1)*512, c2*128:(c2+1)*128],
                               transpose=True, wait=w)
                    p.wait_tok("tensor", at)
                    p.inc("mm")
                    mmB.append(p.cnt("mm"))
                    p.emit("tensor", lambda E, o=ps_mm[0:dv, cpar], l=vsb[:, c2, 0:dv],
                           r=attn_tsb[:, c2 % 2, :], s0=(c2 == 0), s1=(c2 == used_c - 1):
                           E.matmul(o, l, r, start=s0, stop=s1).then_inc(p.semh["mm"], 1))
                p.wait("scalar", "mm", p.cnt("mm"))
                if len(ostore_toks) >= 2:
                    p.wait_tok("scalar", ostore_toks[-2])
                p.inc("ac")
                p.emit("scalar", lambda E, o=osb[0:dv, cpar, :], i=ps_mm[0:dv, cpar]:
                       E.activation(o, i, AF.Identity).then_inc(p.semh["ac"], 1))
                p.wait("gpsimd", "ac", p.cnt("ac"))
                ostore_toks.append(
                    p.dma("gpsimd", outT_dram[h*dv:(h+1)*dv, b*t_seq + qc*512: b*t_seq + (qc+1)*512],
                          osb[0:dv, cpar, :]))
            mm_end_prev = p.cnt("mm")
            bh += 1
    p.barrier()


def emit_add3(p, uid, out_f32, x_f32, ctxT_bf16, xn2_f32, ntok=NTOK, d=D):
    p.arena_reset()
    xs = p.abuf("a3_x", [128, 2, d], F32)
    cs = p.abuf("a3_c", [128, 2, d], BF16)
    ns = p.abuf("a3_n", [128, 2, d], F32)
    os = p.abuf("a3_o", [128, 2, d], F32)
    nt = ntok // 128
    stoks = []
    for t in range(nt):
        tb = t % 2
        w = stoks[t - 2] if t >= 2 else None
        lx = p.dma("sync", xs[:, tb], x_f32[t*128:(t+1)*128, :], wait=w)
        lc = p.dma("sync", cs[:, tb], ctxT_bf16[0:d, t*128:(t+1)*128], transpose=True)
        ln_ = p.dma("gpsimd", ns[:, tb], xn2_f32[t*128:(t+1)*128, :])
        p.wait_tok("vector", lx)
        p.wait_tok("vector", lc)
        p.inc("ev")
        p.emit("vector", lambda E, tb=tb: E.tensor_add(os[:, tb], xs[:, tb], cs[:, tb]).then_inc(p.semh["ev"], 1))
        p.wait_tok("gpsimd", ln_)
        p.wait("gpsimd", "ev", p.cnt("ev"))
        p.inc("po")
        p.emit("gpsimd", lambda E, tb=tb: E.tensor_add(os[:, tb], os[:, tb], ns[:, tb]).then_inc(p.semh["po"], 1))
        stoks.append(p.dma("gpsimd", out_f32[t*128:(t+1)*128, :], os[:, tb], wait=("po", p.cnt("po"))))
    p.barrier()


def _ensure_ident(p):
    if getattr(p, "_ident_done", False):
        return p.sb["ident"]
    import ml_dtypes
    ident_np = np.eye(128, dtype=ml_dtypes.bfloat16)
    t = p.nc.inline_tensor(ident_np, name="ident_const")
    ident = p.sbuf("ident", [128, 128], BF16)
    tok = p.dma("sync", ident.ap()[:], t.ap())
    p.wait_tok("tensor", tok)
    p._ident_done = True
    return ident


def emit_gates_l0(p, uid, logits_vT, logits_oT, rs_dram, gate_vT, gate_oT, ntok=NTOK):
    ident = _ensure_ident(p)
    p.arena_reset()
    lv = p.abuf("g_lv", [128, 2, 64], BF16)
    lo = p.abuf("g_lo", [128, 2, 64], BF16)
    pv = p.abuf("g_pv", [128, 2, 64], F32)
    po_ = p.abuf("g_po", [128, 2, 64], F32)
    w1 = p.abuf("g_w1", [128, 2, 2, 64], F32)
    m_ = p.abuf("g_m", [128, 2, 2, 8, 4], F32)
    gv16 = p.abuf("g_gv16", [128, 2, 64], BF16)
    go16 = p.abuf("g_go16", [128, 2, 64], BF16)
    ot = p.abuf("g_ot", [128, 2, 2, 128], BF16)
    rs_sb = p.abuf("g_rs", [128, 1], F32)
    ps_t = p.psum("ps_sc", [128, 4, 512], F32)
    pst_v = ps_t.ap()[:, 0, 0:128].bitcast(BF16)[0:128, 0:128]
    pst_o = ps_t.ap()[:, 1, 0:128].bitcast(BF16)[0:128, 0:128]
    rtok = bcast_row(p, rs_sb[:], rs_dram)
    p.wait_tok("vector", rtok)
    nt = ntok // 128
    stoks = []
    for t in range(nt):
        tb = t % 2
        if t >= 2:
            p.wait_toks("sync", stoks[t - 2])
        l1 = p.dma("sync", lv[:, tb], logits_vT[0:64, t*128:(t+1)*128], transpose=True)
        l2 = p.dma("sync", lo[:, tb], logits_oT[0:64, t*128:(t+1)*128], transpose=True)
        p.wait_tok("scalar", l1)
        p.wait_tok("scalar", l2)
        p.inc("ac", 2)
        p.emit("scalar", lambda E, tb=tb: E.activation(pv[:, tb], lv[:, tb], AF.Sigmoid).then_inc(p.semh["ac"], 1))
        p.emit("scalar", lambda E, tb=tb: E.activation(po_[:, tb], lo[:, tb], AF.Sigmoid).then_inc(p.semh["ac"], 1))
        p.wait("vector", "ac", p.cnt("ac"))
        for j, pr in ((0, pv), (1, po_)):
            pr3 = pr[:, tb].rearrange("p (h e) -> p h e", e=8)
            m1 = m_[:, tb, j, :, 0]
            m2 = m_[:, tb, j, :, 1]
            den = m_[:, tb, j, :, 2]
            sc = m_[:, tb, j, :, 3]
            eq = w1[:, tb, j].rearrange("p (h e) -> p h e", e=8)
            p.inc("ev")
            p.emit("vector", lambda E, o=m1, i=pr3: E.reduce_max(o, i, axis=AX.X).then_inc(p.semh["ev"], 1))
            p.wait("vector", "ev", p.cnt("ev"))
            p.inc("ev")
            p.emit("vector", lambda E, o=eq, i=pr3, m=m1.unsqueeze(2).to_broadcast([128, 8, 8]):
                   E.tensor_tensor(o, i, m, AL.is_equal).then_inc(p.semh["ev"], 1))
            p.wait("vector", "ev", p.cnt("ev"))
            p.inc("ev")
            p.emit("vector", lambda E, o=eq, i=eq, pr3=pr3:
                   E.scalar_tensor_tensor(o, i, -2.0, pr3, AL.mult, AL.add).then_inc(p.semh["ev"], 1))
            p.wait("vector", "ev", p.cnt("ev"))
            p.inc("ev")
            p.emit("vector", lambda E, o=m2, i=eq: E.reduce_max(o, i, axis=AX.X).then_inc(p.semh["ev"], 1))
            p.wait("vector", "ev", p.cnt("ev"))
            p.inc("ev", 2)
            p.emit("vector", lambda E, o=eq, i=pr3, m=m2.unsqueeze(2).to_broadcast([128, 8, 8]):
                   E.tensor_tensor(o, i, m, AL.is_ge).then_inc(p.semh["ev"], 1))
            p.emit("vector", lambda E, o=den, a=m1, b=m2: E.tensor_add(o, a, b).then_inc(p.semh["ev"], 1))
            p.wait("vector", "ev", p.cnt("ev"))
            p.inc("ev")
            p.emit("vector", lambda E, o=den, i=den: E.tensor_scalar(o, i, 1e-9, None, AL.max).then_inc(p.semh["ev"], 1))
            p.wait("vector", "ev", p.cnt("ev"))
            p.inc("ev")
            p.emit("vector", lambda E, o=sc, i=den: E.reciprocal(o, i).then_inc(p.semh["ev"], 1))
            p.wait("vector", "ev", p.cnt("ev"))
            p.inc("ev")
            p.emit("vector", lambda E, o=sc, i=sc, r=rs_sb[:, 0:1]:
                   E.tensor_scalar(o, i, r, None, AL.mult).then_inc(p.semh["ev"], 1))
            p.wait("vector", "ev", p.cnt("ev"))
            p.inc("ev")
            p.emit("vector", lambda E, o=pr3, i=pr3, m=eq: E.tensor_mul(o, i, m).then_inc(p.semh["ev"], 1))
            p.wait("vector", "ev", p.cnt("ev"))
            dst = gv16 if j == 0 else go16
            p.inc("ev")
            p.emit("vector", lambda E, o=dst[:, tb], i=pr3, s=sc.unsqueeze(2).to_broadcast([128, 8, 8]):
                   E.tensor_tensor(o.rearrange("p (h e) -> p h e", e=8), i, s, AL.mult).then_inc(p.semh["ev"], 1))
        evd = p.cnt("ev")
        p.wait("tensor", "ev", evd)
        p.wait("tensor", "ax", p.cnt("ax"))
        p.inc("mm", 2)
        p.emit("tensor", lambda E, tb=tb, o=pst_v[0:64, :]: E.transpose(o, gv16[:, tb], ident.ap()[:]).then_inc(p.semh["mm"], 1))
        p.emit("tensor", lambda E, tb=tb, o=pst_o[0:64, :]: E.transpose(o, go16[:, tb], ident.ap()[:]).then_inc(p.semh["mm"], 1))
        p.wait("scalar", "mm", p.cnt("mm"))
        if t >= 2:
            p.wait_toks("scalar", stoks[t-2])
        p.inc("ax", 2)
        p.emit("scalar", lambda E, tb=tb, i=pst_v: E.activation(ot[0:64, tb, 0], i[0:64, :], AF.Identity).then_inc(p.semh["ax"], 1))
        p.emit("scalar", lambda E, tb=tb, i=pst_o: E.activation(ot[0:64, tb, 1], i[0:64, :], AF.Identity).then_inc(p.semh["ax"], 1))
        p.wait("gpsimd", "ax", p.cnt("ax"))
        s1 = p.dma("gpsimd", gate_vT[0:64, t*128:(t+1)*128], ot[0:64, tb, 0])
        s2 = p.dma("gpsimd", gate_oT[0:64, t*128:(t+1)*128], ot[0:64, tb, 1])
        stoks.append([s1, s2])
    p.barrier()


def emit_gate_combine_fm(p, uid, srcT, gateT, dstT, H=8, E_=8, P_=128, ntok=NTOK,
                         reduce_e=True):
    p.arena_reset()
    gb = p.abuf("gc_gb", [128, 2, ntok], BF16)
    xsb = p.abuf("gc_x", [128, 2, ntok], BF16)
    tm_ = p.abuf("gc_t", [128, 2, ntok], F32)
    acc = p.abuf("gc_a", [128, ntok], F32)
    o16 = p.abuf("gc_o", [128, 2, ntok], BF16)
    stoks = []
    ev_marks = []
    for h in range(H):
        for e in range(E_):
            he = h * E_ + e
            par = he % 2
            wv_ = ("ev", ev_marks[he - 2]) if he >= 2 else None
            g1 = p.dma("sync", gb[:, par], gateT[he:he+1, 0:ntok].partition_broadcast(128), wait=wv_)
            src_rows = srcT[he*P_:(he+1)*P_, 0:ntok] if reduce_e else srcT[h*P_:(h+1)*P_, 0:ntok]
            x1 = p.dma("sync", xsb[:, par], src_rows)
            p.wait_tok("vector", g1)
            p.wait_tok("vector", x1)
            if reduce_e:
                if he >= 2:
                    p.wait("vector", "po", p.cnt("po") - 1)
                p.inc("ev")
                ev_marks.append(p.cnt("ev"))
                p.emit("vector", lambda E_v, par=par: E_v.tensor_mul(tm_[:, par], xsb[:, par], gb[:, par]).then_inc(p.semh["ev"], 1))
                p.wait("gpsimd", "ev", p.cnt("ev"))
                if e == 0:
                    p.wait("gpsimd", "ac", p.cnt("ac"))
                    p.inc("po")
                    p.emit("gpsimd", lambda E_v, par=par: E_v.tensor_scalar(acc[:], tm_[:, par], 1.0, None, AL.mult).then_inc(p.semh["po"], 1))
                else:
                    p.wait("gpsimd", "po", p.cnt("po"))
                    p.inc("po")
                    p.emit("gpsimd", lambda E_v, par=par: E_v.tensor_add(acc[:], acc[:], tm_[:, par]).then_inc(p.semh["po"], 1))
                if e == E_ - 1:
                    p.wait("scalar", "po", p.cnt("po"))
                    if len(stoks) >= 1:
                        p.wait_toks("scalar", stoks[-1:])
                    p.inc("ac")
                    p.emit("scalar", lambda E_v, par=par: E_v.activation(o16[:, par], acc[:], AF.Identity).then_inc(p.semh["ac"], 1))
                    p.wait("gpsimd", "ac", p.cnt("ac"))
                    stoks.append(p.dma("gpsimd", dstT[h*P_:(h+1)*P_, 0:ntok], o16[:, par]))
            else:
                if len(stoks) >= 2:
                    p.wait_toks("vector", [stoks[-2]])
                p.inc("ev")
                ev_marks.append(p.cnt("ev"))
                p.emit("vector", lambda E_v, par=par: E_v.tensor_mul(o16[:, par], xsb[:, par], gb[:, par]).then_inc(p.semh["ev"], 1))
                p.wait("gpsimd", "ev", p.cnt("ev"))
                stoks.append(p.dma("gpsimd", dstT[he*P_:(he+1)*P_, 0:ntok], o16[:, par]))
    p.barrier()


def emit_pscore(p, uid, qvT_dram, rT_dram, p_scr, H=16, dq=64, ntok=NTOK, t_seq=T):
    nb = ntok // t_seq
    nqt = t_seq // 128
    nkc = t_seq // 512
    p.arena_reset()
    qsb = p.abuf("pp_q", [128, t_seq], BF16)
    rsb_ = p.abuf("pp_r", [128, t_seq], BF16)
    ob = p.abuf("pp_o", [128, 2, t_seq], BF16)
    ps_sc = p.psum("ps_sc", [128, 4, 512], F32)
    sc_flat = ps_sc.ap().rearrange("p a b -> p (a b)")
    mm_prev = p.cnt("mm")
    stoks = []
    for b in range(nb):
        for h in range(H):
            ql = p.dma("sync", qsb[0:dq, 0:t_seq], qvT_dram[h*dq:(h+1)*dq, b*t_seq:(b+1)*t_seq],
                       wait=("mm", mm_prev))
            rl = p.dma("sync", rsb_[0:dq, 0:t_seq], rT_dram[h*dq:(h+1)*dq, 0:t_seq])
            p.wait_tok("tensor", ql)
            p.wait_tok("tensor", rl)
            for qt in range(nqt):
                qpar = qt % 2
                p.wait("tensor", "ax", p.cnt("ax"))
                for kc in range(nkc):
                    p.inc("mm")
                    p.emit("tensor", lambda E, o=ps_sc[:, kc], l=qsb[0:dq, qt*128:(qt+1)*128],
                           r=rsb_[0:dq, kc*512:(kc+1)*512]:
                           E.matmul(o, l, r, start=True, stop=True).then_inc(p.semh["mm"], 1))
                p.wait("scalar", "mm", p.cnt("mm"))
                if len(stoks) >= 2:
                    p.wait_toks("scalar", [stoks[-2]])
                p.inc("ax")
                p.emit("scalar", lambda E, o=ob[:, qpar, :], i=sc_flat[:, 0:t_seq]:
                       E.activation(o, i, AF.Identity).then_inc(p.semh["ax"], 1))
                p.wait("gpsimd", "ax", p.cnt("ax"))
                stoks.append(p.dma("gpsimd", p_scr[b, h, qt*128:(qt+1)*128, :], ob[:, qpar, :]))
            mm_prev = p.cnt("mm")
    p.barrier()


def emit_moe_gate(p, uid, logitsT, gates_out, NE=8, ntok=NTOK):
    p.arena_reset()
    ls = p.abuf("mg_l", [128, 2, 16], BF16)
    lf = p.abuf("mg_lf", [128, 2, NE], F32)
    ex = p.abuf("mg_e", [128, 2, NE], F32)
    m_ = p.abuf("mg_m", [128, 2, 6], F32)
    g_ = p.abuf("mg_g", [128, 2, NE], F32)
    nt = ntok // 128
    stoks = []
    for t in range(nt):
        tb = t % 2
        if t >= 2:
            p.wait_toks("sync", [stoks[t-2]])
        l1 = p.dma("sync", ls[:, tb], logitsT[0:16, t*128:(t+1)*128], transpose=True)
        p.wait_tok("vector", l1)
        p.inc("ev")
        p.emit("vector", lambda E, tb=tb: E.tensor_copy(lf[:, tb], ls[:, tb, 0:NE]).then_inc(p.semh["ev"], 1))
        p.wait("vector", "ev", p.cnt("ev"))
        p.inc("ev")
        p.emit("vector", lambda E, tb=tb: E.reduce_max(m_[:, tb, 0:1], lf[:, tb], axis=AX.X, negate=True).then_inc(p.semh["ev"], 1))
        p.wait("scalar", "ev", p.cnt("ev"))
        p.inc("ac")
        p.emit("scalar", lambda E, tb=tb: E.activation(ex[:, tb], lf[:, tb], AF.Exp,
               bias=m_[:, tb, 0:1], accum_out=m_[:, tb, 1:2]).then_inc(p.semh["ac"], 1))
        p.wait("vector", "ac", p.cnt("ac"))
        p.inc("ev")
        p.emit("vector", lambda E, tb=tb: E.reciprocal(m_[:, tb, 2:3], m_[:, tb, 1:2]).then_inc(p.semh["ev"], 1))
        p.wait("vector", "ev", p.cnt("ev"))
        p.inc("ev")
        p.emit("vector", lambda E, tb=tb: E.tensor_scalar(m_[:, tb, 3:4], m_[:, tb, 0:1], -1.0, None, AL.mult).then_inc(p.semh["ev"], 1))
        p.wait("vector", "ev", p.cnt("ev"))
        p.inc("ev")
        p.emit("vector", lambda E, tb=tb: E.tensor_scalar(g_[:, tb], lf[:, tb], m_[:, tb, 3:4], None, AL.is_ge).then_inc(p.semh["ev"], 1))
        p.wait("vector", "ev", p.cnt("ev"))
        p.inc("ev")
        p.emit("vector", lambda E, tb=tb: E.tensor_scalar(g_[:, tb], g_[:, tb], m_[:, tb, 2:3], None, AL.mult).then_inc(p.semh["ev"], 1))
        p.wait("gpsimd", "ev", p.cnt("ev"))
        stoks.append(p.dma("gpsimd", gates_out[t*128:(t+1)*128, :], g_[:, tb]))
    p.barrier()



def emit_s7_conv_dt(p, uid, xbcdtT, xbcT, Pf_d, Pb_d, rdts_d,
                    conv_w_d, conv_b_d, dt_bias_d, A_log_d, ntok=NTOK):
    p.arena_reset()
    dt_sb = p.abuf("c_dt", [32, ntok], F32)
    dA_sb = p.abuf("c_dA", [32, ntok], F32)
    P_sb = p.abuf("c_P", [32, ntok], F32)
    rdts_sb = p.abuf("c_rdts", [32, ntok], F32)
    ones_sb = p.abuf("c_ones", [32, 128], F32)
    p.inc("ev")
    p.emit("vector", lambda E: E.memset(ones_sb[0:32, :], 1.0).then_inc(p.semh["ev"], 1))
    xs = p.abuf("c_x", [128, 2, T], BF16)
    a0 = p.abuf("c_a0", [128, 2, T], F32)
    a1 = p.abuf("c_a1", [128, 2, T], F32)
    ob = p.abuf("c_o", [128, 2, T], BF16)
    wsb = p.abuf("c_w", [128, 18, 4], F32)
    bsb = p.abuf("c_b", [128, 18], F32)
    dtb = p.abuf("c_dtb", [32, 1], F32)
    nA = p.abuf("c_nA", [32, 1], F32)
    tmp32 = p.abuf("c_t32", [32, 2, ntok], F32)
    dtr = p.abuf("c_dtr", [32, ntok], BF16)
    wt = p.dma("gpsimd", wsb[:, :, :].squeeze() if False else wsb[:, :, :], conv_w_d.rearrange("(c p) k -> p c k", p=128))
    bt = p.dma("sync", bsb[:], conv_b_d.rearrange("(c p) -> p c", p=128), allow_slow=True)
    p.wait_tok("vector", wt)
    p.wait_tok("scalar", bt)
    stoks = []
    for ci in range(18):
        for b in range(B_LOC):
            it = ci * 2 + b
            par = it % 2
            if it >= 2:
                p.wait_toks("sync", [stoks[it-2]])
            lx = p.dma("sync", xs[:, par], xbcdtT[ci*128:(ci+1)*128, b*T:(b+1)*T])
            p.wait_tok("vector", lx)
            p.inc("ev")
            p.emit("vector", lambda E, par=par, ci=ci: E.tensor_scalar(
                a0[:, par], xs[:, par], wsb[:, ci, 3:4], None, AL.mult).then_inc(p.semh["ev"], 1))
            bufs = [a0, a1]
            for si, s in enumerate((1, 2, 3)):
                src_b, dst_b = bufs[si % 2], bufs[(si + 1) % 2]
                k = 3 - s
                p.wait("vector", "ev", p.cnt("ev"))
                p.inc("ev", 2)
                p.emit("vector", lambda E, par=par, s=s, src_b=src_b, dst_b=dst_b:
                       E.tensor_copy(dst_b[:, par, 0:s], src_b[:, par, 0:s]).then_inc(p.semh["ev"], 1))
                p.emit("vector", lambda E, par=par, ci=ci, s=s, k=k, src_b=src_b, dst_b=dst_b:
                       E.scalar_tensor_tensor(dst_b[:, par, s:T], xs[:, par, 0:T-s],
                                              wsb[:, ci, k:k+1], src_b[:, par, s:T],
                                              AL.mult, AL.add).then_inc(p.semh["ev"], 1))
            fin = bufs[3 % 2]
            p.wait("scalar", "ev", p.cnt("ev"))
            p.inc("ac")
            p.emit("scalar", lambda E, par=par, ci=ci, fin=fin: E.activation(
                ob[:, par], fin[:, par], AF.Silu, bias=bsb[:, ci:ci+1], scale=1.0).then_inc(p.semh["ac"], 1))
            p.wait("gpsimd", "ac", p.cnt("ac"))
            stoks.append(p.dma("gpsimd", xbcT[ci*128:(ci+1)*128, b*T:(b+1)*T], ob[:, par]))
    l1 = p.dma("sync", dtr[0:32, :], xbcdtT[2304:2336, 0:ntok])
    b1 = p.dma("sync", dtb[:], dt_bias_d.rearrange("(p o) -> p o", o=1), allow_slow=True)
    b2 = p.dma("sync", nA[:], A_log_d.rearrange("(p o) -> p o", o=1), allow_slow=True)
    p.wait_tok("vector", l1)
    p.wait_tok("vector", b1)
    p.inc("ev")
    p.emit("vector", lambda E: E.tensor_scalar(tmp32[0:32, 0, :], dtr[0:32, :], dtb[:, 0:1], None, AL.add).then_inc(p.semh["ev"], 1))
    p.wait("scalar", "ev", p.cnt("ev"))
    p.inc("ac")
    p.emit("scalar", lambda E: E.activation(tmp32[0:32, 1, :], tmp32[0:32, 0, :], AF.Abs).then_inc(p.semh["ac"], 1))
    p.wait("scalar", "ac", p.cnt("ac"))
    p.inc("ac")
    p.emit("scalar", lambda E: E.activation(tmp32[0:32, 1, :], tmp32[0:32, 1, :], AF.Exp, scale=-1.0).then_inc(p.semh["ac"], 1))
    p.wait("vector", "ac", p.cnt("ac"))
    p.inc("ev")
    p.emit("vector", lambda E: E.tensor_scalar(tmp32[0:32, 1, :], tmp32[0:32, 1, :], 1.0, None, AL.add).then_inc(p.semh["ev"], 1))
    p.wait("scalar", "ev", p.cnt("ev"))
    p.inc("ac")
    p.emit("scalar", lambda E: E.activation(tmp32[0:32, 1, :], tmp32[0:32, 1, :], AF.Ln).then_inc(p.semh["ac"], 1))
    p.inc("ev")
    p.emit("vector", lambda E: E.tensor_scalar(dt_sb[0:32, :], tmp32[0:32, 0, :], 0.0, None, AL.max).then_inc(p.semh["ev"], 1))
    p.wait("vector", "ac", p.cnt("ac"))
    p.wait("vector", "ev", p.cnt("ev"))
    p.inc("ev")
    p.emit("vector", lambda E: E.tensor_add(dt_sb[0:32, :], dt_sb[0:32, :], tmp32[0:32, 1, :]).then_inc(p.semh["ev"], 1))
    p.wait_tok("scalar", b2)
    p.inc("ac")
    p.emit("scalar", lambda E: E.activation(nA[:, 0:1], nA[:, 0:1], AF.Exp).then_inc(p.semh["ac"], 1))
    p.wait("vector", "ac", p.cnt("ac"))
    p.inc("ev")
    p.emit("vector", lambda E: E.tensor_scalar(nA[:, 0:1], nA[:, 0:1], -1.0, None, AL.mult).then_inc(p.semh["ev"], 1))
    p.wait("vector", "ev", p.cnt("ev"))
    p.inc("ev")
    p.emit("vector", lambda E: E.tensor_scalar(dA_sb[0:32, :], dt_sb[0:32, :], nA[:, 0:1], None, AL.mult).then_inc(p.semh["ev"], 1))
    p.wait("vector", "ev", p.cnt("ev"))
    for b in range(B_LOC):
        for c in range(T // 128):
            p.inc("ev")
            p.emit("vector", lambda E, b=b, c=c: E.tensor_tensor_scan(
                P_sb[0:32, b*T + c*128: b*T + (c+1)*128],
                ones_sb[0:32, 0:128],
                dA_sb[0:32, b*T + c*128: b*T + (c+1)*128], 0.0, AL.mult, AL.add).then_inc(p.semh["ev"], 1))
    p.wait("gpsimd", "ev", p.cnt("ev"))
    p.wait("gpsimd", "ac", p.cnt("ac"))
    p.dma("gpsimd", Pf_d[0:32, :], P_sb[0:32, :])
    p.dma("gpsimd", Pb_d[0:32, :], P_sb[0:32, :])
    p.dma("gpsimd", rdts_d[0:32, :], dt_sb[0:32, :])
    p.barrier()


def emit_s7_scan(p, uid, xbcT, yT, Pf_d, Pb_d, rdts_d, D_vec_d, mask_ut, ident,
                 ntok=NTOK):
    NCH = T // 128
    NH_ = 32
    p.arena_reset()
    bcbt = p.abuf("s_bc", [128, 2, 128], BF16)
    cmt = p.abuf("s_cm", [128, 2, 128], BF16)
    btm = p.abuf("s_bt", [128, 2, 128], BF16)
    xtm = p.abuf("s_xt", [128, 2, 2048], BF16)
    rdt = p.abuf("s_rd", [128, 2, 32], BF16)
    rdt32 = p.abuf("s_rd32", [128, 2, 32], F32)
    lcol = p.abuf("s_lc", [128, 2, 32], BF16)
    lcol32 = p.abuf("s_lc32", [128, 2, 32], F32)
    wall = p.abuf("s_wall", [128, 2, 32], F32)
    pendB = p.abuf("s_pB", [128, 2, 32], F32)
    gam = p.abuf("s_gam", [128, 2, 128], F32)
    pend = p.abuf("s_pe", [128, 2, 32], F32)
    pbc = p.abuf("s_pb", [128, 2, 128], BF16)
    mt = p.abuf("s_mt", [128, 2, 128], F32)
    expl = p.abuf("s_el", [128, 2, 128], F32)
    cmth = p.abuf("s_ch", [128, 2, 128], BF16)
    mprime = p.abuf("s_mp", [128, 2, 128], BF16)
    wcol = p.abuf("s_wc", [128, 2, 1], F32)
    xhat = p.abuf("s_xh", [128, 2, 64], BF16)
    state = p.abuf("s_st", [128, 2, NH_, 64], F32)
    stb = p.abuf("s_sb", [128, NH_, 64], BF16)
    ysb = p.abuf("s_y", [128, 2, 16, 128], BF16)
    db = p.abuf("s_db", [128, 32], F32)
    dbr = p.abuf("s_dr", [1, 32], BF16)
    onesb = p.abuf("s_on", [1, 128], BF16)
    identD = p.abuf("s_id", [128, NH_, 128], BF16)
    ps_sc = p.psum("ps_sc", [128, 4, 512], F32)
    ps_y = p.psum("ps_mm", [128, 2, 512], F32)
    cbt_b = [ps_sc[:, 0, 0:128], ps_sc[:, 1, 0:128]]
    sct_b = [ps_sc[:, 2, 0:64], ps_sc[:, 3, 0:64]]
    y_b = [ps_y[:, 0, 0:128], ps_y[:, 1, 0:128]]

    p.inc("ev")
    p.emit("vector", lambda E: E.memset(onesb[:], 1.0).then_inc(p.semh["ev"], 1))
    dload = p.dma("gpsimd", dbr[:], D_vec_d.rearrange("(o n) -> o n", o=1))
    p.wait_tok("tensor", dload)
    p.wait("tensor", "ev", p.cnt("ev"))
    p.inc("mm")
    p.emit("tensor", lambda E: E.matmul(ps_sc[:, 0, 0:32], onesb[:], dbr[:], start=True, stop=True).then_inc(p.semh["mm"], 1))
    p.wait("vector", "mm", p.cnt("mm"))
    p.inc("ev")
    p.emit("vector", lambda E: E.tensor_scalar(db[:], ps_sc[:, 0, 0:32], 1.0, None, AL.mult).then_inc(p.semh["ev"], 1))
    p.wait("vector", "ev", p.cnt("ev"))
    for h in range(NH_):
        p.inc("ev")
        p.emit("vector", lambda E, h=h: E.tensor_scalar(identD[:, h], ident.ap()[:], db[:, h:h+1], None, AL.mult).then_inc(p.semh["ev"], 1))

    ystores = []
    for b in range(B_LOC):
        p.wait("tensor", "ax", p.cnt("ax"))
        p.wait("vector", "ax", p.cnt("ax"))
        p.wait("vector", "ev", p.cnt("ev"))
        p.inc("ev", 2)
        p.emit("vector", lambda E: E.memset(state[:, 0], 0.0).then_inc(p.semh["ev"], 1))
        p.emit("vector", lambda E: E.memset(stb[:], 0.0).then_inc(p.semh["ev"], 1))
        st_init_ev = p.cnt("ev")
        p.wait("tensor", "ev", st_init_ev)
        for c in range(NCH):
            par = c % 2
            spar = 0
            cs = slice(b*T + c*128, b*T + (c+1)*128)
            p.wait("sync", "mm", p.cnt("mm"))
            p.wait("sync", "ev", p.cnt("ev"))
            p.wait("sync", "po", p.cnt("po"))
            l1 = p.dma("sync", bcbt[:, par], xbcT[2048:2176, cs])
            l2 = p.dma("sync", cmt[:, par], xbcT[2176:2304, cs])
            l3 = p.dma("sync", btm[:, par], xbcT[2048:2176, cs], transpose=True)
            l4 = p.dma("sync", xtm[:, par], xbcT[0:2048, cs], transpose=True)
            l5 = p.dma("sync", rdt[:, par], rdts_d[0:32, cs], transpose=True)
            l5b = p.dma("sync", lcol[:, par], Pb_d[0:32, cs], transpose=True)
            l6 = p.dma("sync", pend[:, par],
                       Pf_d[0:32, b*T + c*128 + 127: b*T + (c+1)*128].rearrange("h o -> o h").partition_broadcast(128),
                       allow_slow=True)
            p.wait_toks("tensor", [l1, l2])
            p.inc("mm")
            p.emit("tensor", lambda E, par=par: E.matmul(cbt_b[par], bcbt[:, par], cmt[:, par],
                   start=True, stop=True).then_inc(p.semh["mm"], 1))
            cbt_mm = p.cnt("mm")
            p.wait_toks("vector", [l5, l5b, l6])
            p.wait_toks("gpsimd", [l6])
            p.inc("ev", 2)
            p.emit("vector", lambda E, par=par: E.tensor_copy(rdt32[:, par], rdt[:, par]).then_inc(p.semh["ev"], 1))
            p.emit("vector", lambda E, par=par: E.tensor_copy(lcol32[:, par], lcol[:, par]).then_inc(p.semh["ev"], 1))
            p.wait("vector", "ev", p.cnt("ev"))
            p.wait_toks("vector", [l4])
            p.inc("ev")
            p.emit("vector", lambda E, par=par: E.tensor_sub(wall[:, par], pend[:, par], lcol32[:, par]).then_inc(p.semh["ev"], 1))
            p.wait("scalar", "ev", p.cnt("ev"))
            p.inc("ac", 2)
            p.emit("scalar", lambda E, par=par: E.activation(pendB[:, par], pend[:, par], AF.Exp).then_inc(p.semh["ac"], 1))
            p.emit("scalar", lambda E, par=par: E.activation(wall[:, par], wall[:, par], AF.Exp).then_inc(p.semh["ac"], 1))
            p.wait("vector", "ac", p.cnt("ac"))
            p.inc("ev")
            p.emit("vector", lambda E, par=par: E.tensor_mul(wall[:, par], wall[:, par], rdt32[:, par]).then_inc(p.semh["ev"], 1))
            for h in range(NH_):
                hp = h % 2
                lp = p.dma("scalar", pbc[:, hp], Pb_d[h:h+1, cs].partition_broadcast(128),
                           wait=("po", p.cnt("po")))
                p.wait("vector", "mm", p.cnt("mm"))
                if h >= 2:
                    p.wait("vector", "po", p.cnt("po") - 1)
                p.wait_tok("vector", lp)
                p.inc("ev")
                p.emit("vector", lambda E, par=par, hp=hp, h=h: E.tensor_scalar(
                    gam[:, hp], pbc[:, hp], lcol32[:, par, h:h+1], None, AL.subtract).then_inc(p.semh["ev"], 1))
                p.wait("vector", "ev", p.cnt("ev"))
                p.inc("ev")
                p.emit("vector", lambda E, hp=hp: E.tensor_add(gam[:, hp], gam[:, hp], mask_ut.ap()[:]).then_inc(p.semh["ev"], 1))
                ev_g = p.cnt("ev")
                p.wait("scalar", "ev", ev_g)
                p.wait_tok("scalar", lp)
                if h >= 2:
                    p.wait("scalar", "po", p.cnt("po"))
                p.inc("ac", 2)
                p.emit("scalar", lambda E, hp=hp: E.activation(gam[:, hp], gam[:, hp], AF.Exp).then_inc(p.semh["ac"], 1))
                p.emit("scalar", lambda E, hp=hp: E.activation(expl[:, hp], pbc[:, hp], AF.Exp).then_inc(p.semh["ac"], 1))
                ac_g = p.cnt("ac")
                p.inc("ev")
                p.emit("vector", lambda E, par=par, hp=hp, h=h: E.tensor_scalar(
                    xhat[:, hp], xtm[:, par, h*64:(h+1)*64], wall[:, par, h:h+1], None, AL.mult).then_inc(p.semh["ev"], 1))
                ev_xh = p.cnt("ev")
                p.wait("vector", "ac", ac_g)
                p.inc("ev")
                p.emit("vector", lambda E, par=par, hp=hp: E.tensor_mul(
                    mt[:, hp], cbt_b[par], gam[:, hp]).then_inc(p.semh["ev"], 1))
                ev_mt = p.cnt("ev")
                p.wait("gpsimd", "ev", ev_mt)
                p.wait("gpsimd", "ac", ac_g)
                if h >= 2:
                    p.wait("gpsimd", "mm", p.cnt("mm") - 4)
                p.inc("po", 2)
                p.emit("gpsimd", lambda E, par=par, hp=hp, h=h: E.tensor_scalar(
                    mprime[:, hp], mt[:, hp], rdt32[:, par, h:h+1], None, AL.mult).then_inc(p.semh["po"], 1))
                p.emit("gpsimd", lambda E, par=par, hp=hp: E.tensor_mul(
                    cmth[:, hp], cmt[:, par], expl[:, hp]).then_inc(p.semh["po"], 1))
                po_mp = p.cnt("po")
                p.wait("tensor", "po", po_mp)
                p.wait_toks("tensor", [l4, l3])
                p.wait("tensor", "ev", ev_xh)
                p.wait("tensor", "ac", p.cnt("ac") - 1 if (c > 0 or h >= 2 or b > 0) else 0)
                p.inc("mm", 4)
                p.emit("tensor", lambda E, par=par, hp=hp, h=h: E.matmul(
                    y_b[hp][0:64, :], xtm[:, par, h*64:(h+1)*64], mprime[:, hp], start=True, stop=False).then_inc(p.semh["mm"], 1))
                p.emit("tensor", lambda E, par=par, hp=hp, h=h: E.matmul(
                    y_b[hp][0:64, :], stb[:, h], cmth[:, hp], start=False, stop=False).then_inc(p.semh["mm"], 1))
                p.emit("tensor", lambda E, par=par, hp=hp, h=h: E.matmul(
                    y_b[hp][0:64, :], xtm[:, par, h*64:(h+1)*64], identD[:, h], start=False, stop=True).then_inc(p.semh["mm"], 1))
                p.emit("tensor", lambda E, par=par, hp=hp: E.matmul(
                    sct_b[hp], btm[:, par], xhat[:, hp], start=True, stop=True).then_inc(p.semh["mm"], 1))
                mm_h = p.cnt("mm")
                p.wait("vector", "mm", mm_h)
                p.inc("ev")
                p.emit("vector", lambda E, par=par, hp=hp, h=h, spar=spar: E.scalar_tensor_tensor(
                    state[:, 1 - spar, h], state[:, spar, h], pendB[:, par, h:h+1], sct_b[hp],
                    AL.mult, AL.add).then_inc(p.semh["ev"], 1))
                ev_st = p.cnt("ev")
                p.wait("scalar", "mm", mm_h)
                if h == 0 and len(ystores) >= 2:
                    p.wait_tok("scalar", ystores[-2])
                p.inc("ac")
                p.emit("scalar", lambda E, hp=hp, h=h, par=par: E.activation(
                    ysb[(h % 2)*64:(h % 2)*64 + 64, par, h // 2, :], y_b[hp][0:64, :], AF.Identity).then_inc(p.semh["ac"], 1))
            ev_all = p.cnt("ev")
            p.wait("scalar", "ev", ev_all)
            for h2 in range(NH_):
                p.inc("ax")
                p.emit("scalar", lambda E, h2=h2, spar=spar: E.activation(
                    stb[:, h2], state[:, 1 - spar, h2], AF.Identity).then_inc(p.semh["ax"], 1))
            p.wait("tensor", "ax", p.cnt("ax"))
            p.wait("vector", "ax", p.cnt("ax"))
            p.inc("ev")
            p.emit("vector", lambda E, spar=spar: E.tensor_copy(state[:, spar], state[:, 1 - spar]).then_inc(p.semh["ev"], 1))
            p.wait("tensor", "ev", p.cnt("ev"))
            p.wait("gpsimd", "ac", p.cnt("ac"))
            ystores.append(p.dma("gpsimd", yT[0:2048, cs].rearrange("(a p) t -> p a t", p=128), ysb[:, par]))
    p.barrier()


def emit_s7_gatenorm(p, uid, yT, zT, mo_tm, gnw_d, ntok=NTOK, din=2048):
    p.arena_reset()
    ysb = p.abuf("gn_y", [128, 2, din], BF16)
    zsb = p.abuf("gn_z", [128, 2, din], BF16)
    zs = p.abuf("gn_zs", [128, din], F32)
    yz = p.abuf("gn_yz", [128, 2, din], F32)
    sq = p.abuf("gn_sq", [128, din], F32)
    stt_ = p.abuf("gn_st", [128, 8], F32)
    gw = p.abuf("gn_gw", [128, din], F32)
    o16 = p.abuf("gn_o", [128, 2, din], BF16)
    gl = bcast_row(p, gw[:], gnw_d)
    p.wait_toks("vector", [gl])
    nt = ntok // 128
    stoks = []
    inv_d = 1.0 / din
    for t in range(nt):
        tb = t % 2
        if t >= 2:
            p.wait_toks("sync", [stoks[t-2]])
        ly, lz = [], None
        for k in range(din // 128):
            ly.append(p.dma("sync", ysb[:, tb, k*128:(k+1)*128], yT[k*128:(k+1)*128, t*128:(t+1)*128], transpose=True))
        for k in range(din // 128):
            ly.append(p.dma("scalar", zsb[:, tb, k*128:(k+1)*128], zT[k*128:(k+1)*128, t*128:(t+1)*128], transpose=True))
        p.wait_toks("scalar", ly)
        p.inc("ac")
        p.emit("scalar", lambda E, tb=tb: E.activation(zs[:], zsb[:, tb], AF.Silu).then_inc(p.semh["ac"], 1))
        p.wait("vector", "ac", p.cnt("ac"))
        p.wait_toks("vector", ly)
        p.inc("ev")
        p.emit("vector", lambda E, tb=tb: E.tensor_mul(yz[:, tb], ysb[:, tb], zs[:]).then_inc(p.semh["ev"], 1))
        p.wait("scalar", "ev", p.cnt("ev"))
        p.inc("ac")
        p.emit("scalar", lambda E, tb=tb: E.activation(sq[:], yz[:, tb], AF.Square, accum_out=stt_[:, 0:1]).then_inc(p.semh["ac"], 1))
        p.wait("vector", "ac", p.cnt("ac"))
        p.inc("ev")
        p.emit("vector", lambda E: E.tensor_scalar(stt_[:, 1:2], stt_[:, 0:1], inv_d, 1e-5, AL.mult, AL.add).then_inc(p.semh["ev"], 1))
        p.wait("scalar", "ev", p.cnt("ev"))
        p.inc("ac")
        p.emit("scalar", lambda E: E.activation(stt_[:, 2:3], stt_[:, 1:2], AF.Sqrt).then_inc(p.semh["ac"], 1))
        p.wait("vector", "ac", p.cnt("ac"))
        p.inc("ev")
        p.emit("vector", lambda E: E.reciprocal(stt_[:, 3:4], stt_[:, 2:3]).then_inc(p.semh["ev"], 1))
        p.wait("vector", "ev", p.cnt("ev"))
        p.inc("ev")
        p.emit("vector", lambda E, tb=tb: E.tensor_scalar(yz[:, tb], yz[:, tb], stt_[:, 3:4], None, AL.mult).then_inc(p.semh["ev"], 1))
        p.wait("gpsimd", "ev", p.cnt("ev"))
        p.inc("po")
        p.emit("gpsimd", lambda E, tb=tb: E.tensor_mul(o16[:, tb], yz[:, tb], gw[:]).then_inc(p.semh["po"], 1))
        stoks.append(p.dma("gpsimd", mo_tm[t*128:(t+1)*128, :], o16[:, tb], wait=("po", p.cnt("po"))))
    p.barrier()


def emit_s7_cumsum(p, uid, y1T_f32, y1T_bf, ntok=NTOK, din=2048):
    p.arena_reset()
    a = p.abuf("cs_a", [128, 2, T], F32)
    zz = p.abuf("cs_z", [128, T], F32)
    o = p.abuf("cs_o", [128, 2, T], BF16)
    p.inc("ev")
    p.emit("vector", lambda E: E.memset(zz[:], 0.0).then_inc(p.semh["ev"], 1))
    stoks = []
    for r in range(din // 128):
        for b in range(B_LOC):
            it = r * B_LOC + b
            par = it % 2
            if it >= 2:
                p.wait_toks("sync", [stoks[it-2]])
            l1 = p.dma("sync", a[:, par], y1T_f32[r*128:(r+1)*128, b*T:(b+1)*T])
            p.wait_tok("vector", l1)
            p.wait("vector", "ev", p.cnt("ev"))
            p.inc("ev")
            p.emit("vector", lambda E, par=par: E.tensor_tensor_scan(
                a[:, par], a[:, par], zz[:], 0.0, AL.add, AL.add).then_inc(p.semh["ev"], 1))
            p.wait("scalar", "ev", p.cnt("ev"))
            p.inc("ac")
            p.emit("scalar", lambda E, par=par: E.activation(o[:, par], a[:, par], AF.Identity).then_inc(p.semh["ac"], 1))
            p.wait("gpsimd", "ac", p.cnt("ac"))
            stoks.append(p.dma("gpsimd", y1T_bf[r*128:(r+1)*128, b*T:(b+1)*T], o[:, par]))
    p.barrier()



def emit_s3_av(p, uid, qw_d, qwT_d, vw_d, av_dram):
    p.arena_reset()
    qsb = p.abuf("av_q", [128, 8, 1024], BF16)
    vsb = p.abuf("av_v", [128, 8, 1024], BF16)
    qTsb = p.abuf("av_qT", [128, 8, 1024], BF16)
    m1 = p.abuf("av_m1", [128, 8, 64], BF16)
    osb = p.abuf("av_o", [128, 8, 1024], F32)
    ps = p.psum("ps_mm", [128, 2, 512], F32)
    l1 = p.dma("gpsimd", qsb[:], qw_d.rearrange("(k p) n -> p k n", p=128))
    l2 = p.dma("gpsimd", vsb[:], vw_d.rearrange("(k p) n -> p k n", p=128))
    l3 = p.dma("gpsimd", qTsb[:], qwT_d.rearrange("(k p) n -> p k n", p=128))
    p.wait_toks("tensor", [l1, l2, l3])
    for h in range(16):
        bank = h % 2
        if h >= 2:
            p.wait("tensor", "ac", p.cnt("ac") - 1)
        for k in range(8):
            p.inc("mm") if k == 7 else None
            p.emit("tensor", lambda E, h=h, k=k, bank=bank, fin=(k == 7): E.matmul(
                ps[0:64, bank, 0:64], qsb[:, k, h*64:(h+1)*64], vsb[:, k, h*64:(h+1)*64],
                start=(k == 0), stop=(k == 7)).then_inc(p.semh["mm"], 1) if fin else E.matmul(
                ps[0:64, bank, 0:64], qsb[:, k, h*64:(h+1)*64], vsb[:, k, h*64:(h+1)*64],
                start=(k == 0), stop=(k == 7)))
        p.wait("scalar", "mm", p.cnt("mm"))
        p.inc("ac")
        pr1 = (h % 2) * 64
        p.emit("scalar", lambda E, h=h, bank=bank, pr1=pr1: E.activation(
            m1[pr1:pr1+64, h // 2, :], ps[0:64, bank, 0:64], AF.Identity).then_inc(p.semh["ac"], 1))
    for mchunk in range(8):
        for h in range(16):
            gi = mchunk * 16 + h
            bank = gi % 2
            p.wait("tensor", "ac", p.cnt("ac") if gi < 2 else p.cnt("ac") - 1)
            p.inc("mm")
            pr = (h % 2) * 64
            kc = h // 2
            p.emit("tensor", lambda E, h=h, mchunk=mchunk, bank=bank, pr=pr, kc=kc: E.matmul(
                ps[:, bank, 0:64], qTsb[pr:pr+64, kc, mchunk*128:(mchunk+1)*128], m1[pr:pr+64, h // 2, :],
                start=True, stop=True).then_inc(p.semh["mm"], 1))
            p.wait("scalar", "mm", p.cnt("mm"))
            p.inc("ac")
            p.emit("scalar", lambda E, mchunk=mchunk, h=h, bank=bank: E.activation(
                osb[:, mchunk, h*64:(h+1)*64], ps[:, bank, 0:64], AF.Identity, scale=0.125).then_inc(p.semh["ac"], 1))
    p.wait("gpsimd", "ac", p.cnt("ac"))
    p.dma("gpsimd", av_dram.rearrange("(k p) n -> p k n", p=128), osb[:])
    p.barrier()


def emit_s3_qsq(p, uid, q_tm, qsq_tm, ntok=NTOK):
    p.arena_reset()
    qs = p.abuf("qs_q", [128, 2, 1024], BF16)
    sq = p.abuf("qs_s", [128, 2, 1024], F32)
    o = p.abuf("qs_o", [128, 2, 16], F32)
    stoks = []
    for t in range(ntok // 128):
        tb = t % 2
        if t >= 2:
            p.wait_toks("sync", [stoks[t-2]])
        l = p.dma("sync", qs[:, tb], q_tm[t*128:(t+1)*128, :])
        p.wait_tok("scalar", l)
        p.inc("ac")
        p.emit("scalar", lambda E, tb=tb: E.activation(sq[:, tb], qs[:, tb], AF.Square).then_inc(p.semh["ac"], 1))
        p.wait("vector", "ac", p.cnt("ac"))
        p.inc("ev")
        p.emit("vector", lambda E, tb=tb: E.reduce_sum(
            o[:, tb], sq[:, tb].rearrange("p (h d) -> p h d", d=64), axis=AX.X).then_inc(p.semh["ev"], 1))
        p.wait("gpsimd", "ev", p.cnt("ev"))
        stoks.append(p.dma("gpsimd", qsq_tm[t*128:(t+1)*128, :], o[:, tb]))
    p.barrier()


def emit_s3_mix(p, uid, q_tm, qsq_tm, qall, qsqall, xavall, pxav_tm, ntok=NTOK):
    p.arena_reset()
    qo = p.abuf("mx_qo", [128, 1024], BF16)
    qso = p.abuf("mx_qso", [128, 16], F32)
    qs_ = p.abuf("mx_qs", [128, 2, 1024], BF16)
    qss = p.abuf("mx_qss", [128, 2, 16], F32)
    xav = p.abuf("mx_xav", [128, 2, 1024], BF16)
    prod = p.abuf("mx_pr", [128, 2, 1024], F32)
    pd = p.abuf("mx_pd", [128, 2, 16], F32)
    dbuf = p.abuf("mx_db", [128, 16, 16], F32)
    st = p.abuf("mx_st", [128, 16, 4], F32)
    ex = p.abuf("mx_ex", [128, 16, 16], F32)
    tmp = p.abuf("mx_tmp", [128, 2, 1024], F32)
    acc = p.abuf("mx_acc", [128, 1024], F32)
    o16 = p.abuf("mx_o16", [128, 1024], BF16)
    stoks = []
    for bo in range(B_LOC):
        for tt in range(T // 128):
            row0 = bo * T + tt * 128
            if stoks:
                p.wait_toks("sync", stoks[-2:])
            lo1 = p.dma("sync", qo[:], q_tm[row0:row0+128, :])
            lo2 = p.dma("sync", qso[:], qsq_tm[row0:row0+128, :])
            for s in range(16):
                sp = s % 2
                grow = s * T + tt * 128
                w = ("ev", p.cnt("ev") - 2) if s >= 2 else None
                l1 = p.dma("sync", qs_[:, sp], qall[grow:grow+128, :], wait=w)
                l2 = p.dma("sync", qss[:, sp], qsqall[grow:grow+128, :])
                p.wait_toks("vector", [lo1, l1])
                p.inc("ev")
                p.emit("vector", lambda E, sp=sp: E.tensor_mul(prod[:, sp], qo[:], qs_[:, sp]).then_inc(p.semh["ev"], 1))
                p.wait("vector", "ev", p.cnt("ev"))
                p.inc("ev")
                p.emit("vector", lambda E, sp=sp: E.reduce_sum(
                    pd[:, sp], prod[:, sp].rearrange("p (h d) -> p h d", d=64), axis=AX.X).then_inc(p.semh["ev"], 1))
                p.wait("vector", "ev", p.cnt("ev"))
                p.wait_toks("vector", [lo2])
                p.inc("ev")
                p.emit("vector", lambda E, sp=sp: E.scalar_tensor_tensor(
                    pd[:, sp], pd[:, sp], -2.0, qso[:], AL.mult, AL.add).then_inc(p.semh["ev"], 1))
                p.wait_toks("gpsimd", [l2])
                p.wait("gpsimd", "ev", p.cnt("ev"))
                p.inc("po")
                p.emit("gpsimd", lambda E, sp=sp, s=s: E.tensor_add(
                    dbuf[:, :, s], pd[:, sp], qss[:, sp]).then_inc(p.semh["po"], 1))
            p.wait("vector", "po", p.cnt("po"))
            p.inc("ev")
            p.emit("vector", lambda E: E.tensor_scalar(dbuf[:], dbuf[:], -0.125, None, AL.mult).then_inc(p.semh["ev"], 1))
            p.wait("vector", "ev", p.cnt("ev"))
            p.inc("ev")
            p.emit("vector", lambda E: E.reduce_max(st[:, :, 0], dbuf[:], axis=AX.X, negate=True).then_inc(p.semh["ev"], 1))
            p.wait("vector", "ev", p.cnt("ev"))
            p.inc("ev")
            p.emit("vector", lambda E: E.tensor_tensor(
                ex[:], dbuf[:], st[:, :, 0:1].to_broadcast([128, 16, 16]), AL.add).then_inc(p.semh["ev"], 1))
            p.wait("scalar", "ev", p.cnt("ev"))
            p.inc("ac")
            p.emit("scalar", lambda E: E.activation(ex[:], ex[:], AF.Exp).then_inc(p.semh["ac"], 1))
            p.wait("vector", "ac", p.cnt("ac"))
            p.inc("ev")
            p.emit("vector", lambda E: E.reduce_sum(st[:, :, 1], ex[:], axis=AX.X).then_inc(p.semh["ev"], 1))
            p.wait("vector", "ev", p.cnt("ev"))
            p.inc("ev")
            p.emit("vector", lambda E: E.reciprocal(st[:, :, 2], st[:, :, 1]).then_inc(p.semh["ev"], 1))
            p.wait("vector", "ev", p.cnt("ev"))
            p.inc("ev")
            p.emit("vector", lambda E: E.tensor_tensor(
                ex[:], ex[:], st[:, :, 2:3].to_broadcast([128, 16, 16]), AL.mult).then_inc(p.semh["ev"], 1))
            ex_ev = p.cnt("ev")
            for s in range(16):
                sp = s % 2
                grow = s * T + tt * 128
                w = ("po", p.cnt("po") - 1) if s >= 2 else None
                l3 = p.dma("sync", xav[:, sp], xavall[grow:grow+128, :], wait=w)
                p.wait_toks("vector", [l3])
                p.wait("vector", "ev", ex_ev)
                if s >= 2:
                    p.wait("vector", "po", p.cnt("po") - 1)
                p.inc("ev")
                p.emit("vector", lambda E, sp=sp, s=s: E.tensor_tensor(
                    tmp[:, sp].rearrange("p (h d) -> p h d", d=64), xav[:, sp].rearrange("p (h d) -> p h d", d=64),
                    ex[:, :, s:s+1].to_broadcast([128, 16, 64]), AL.mult).then_inc(p.semh["ev"], 1))
                p.wait("gpsimd", "ev", p.cnt("ev"))
                p.inc("po")
                if s == 0:
                    p.emit("gpsimd", lambda E, sp=sp: E.tensor_scalar(acc[:], tmp[:, sp], 1.0, None, AL.mult).then_inc(p.semh["po"], 1))
                else:
                    p.wait("gpsimd", "po", p.cnt("po") - 1)
                    p.emit("gpsimd", lambda E, sp=sp: E.tensor_add(acc[:], acc[:], tmp[:, sp]).then_inc(p.semh["po"], 1))
            p.wait("scalar", "po", p.cnt("po"))
            if stoks:
                p.wait_toks("scalar", stoks[-1:])
            p.inc("ac")
            p.emit("scalar", lambda E: E.activation(o16[:], acc[:], AF.Identity).then_inc(p.semh["ac"], 1))
            p.wait("gpsimd", "ac", p.cnt("ac"))
            stoks.append(p.dma("gpsimd", pxav_tm[row0:row0+128, :], o16[:]))
    p.barrier()


def emit_s3_gather(p, uid, srcs_and_dsts):
    for i, (s, d) in enumerate(srcs_and_dsts):
        p.inc("cc")
        p.emit("gpsimd", lambda E, s=s, d=d: E.collective_compute(
            "AllGather", AL.bypass, [[0, 1, 2, 3, 4, 5, 6, 7]], [s.opt()], [d.opt()]).then_inc(p.semh["cc"], 1))
    tgt = p.cnt("cc")
    for e in ENGINES:
        p.wait(e, "cc", tgt)
    p.barrier()



S_ = T
NH6, DH6, FF6, NE6 = 16, 64, 4096, 8


def emit_s1(p, pref, x_in, x_out, W):
    xn = p.dram_scratch(pref + "xn", [NTOK, D], BF16)
    cqT = p.dram_scratch(pref + "cqT", [512, NTOK], BF16)
    ckvT = p.dram_scratch(pref + "ckvT", [64, NTOK], BF16)
    qT = p.dram_scratch(pref + "qT", [1024, NTOK], BF16)
    vtm = p.dram_scratch(pref + "vtm", [NTOK, 1024], BF16)
    yT = p.dram_scratch(pref + "yT", [1024, NTOK], BF16)
    emit_ln(p, pref + "ln", x_in, xn)
    emit_mm_fm(p, pref + "cq", cqT, W["s1_wq_dT"], xn, K=1024, Mout=512, bias_dram=W["s1_bq_d"])
    emit_mm_fm(p, pref + "ckv", ckvT, W["s1_wkv_dT"], xn, K=1024, Mout=64, bias_dram=W["s1_bkv_d"], skip_x_load=True)
    emit_mm_fm(p, pref + "q", qT, W["s1_w_qkT"], cqT, K=512, Mout=1024, bias_dram=W["s1_b_qk"], x_src_is_fm=True)
    emit_mm_tm(p, pref + "v", vtm, ckvT, W["s1_wv_uT"], K=64, Nout=1024, bias_dram=W["s1_bv_u"], x_src_is_fm=True)
    emit_attn(p, pref + "a", qT, ckvT, vtm, yT, H=16, dq=64, dv=64, scale=0.125, k_shared=True)
    emit_mm_tm(p, pref + "o", None, yT, W["s1_w_projT"], K=1024, Nout=1024, bias_dram=W["s1_b_proj"],
               resid_dram=x_in, resid_out_f32=x_out, x_src_is_fm=True)


def emit_s2(p, pref, x_in, x_out, W):
    xn = p.dram_scratch(pref + "xn", [NTOK, D], BF16)
    lvT = p.dram_scratch(pref + "lv", [64, NTOK], BF16)
    loT = p.dram_scratch(pref + "lo", [64, NTOK], BF16)
    gvT = p.dram_scratch(pref + "gv", [64, NTOK], BF16)
    goT = p.dram_scratch(pref + "go", [64, NTOK], BF16)
    xvT = p.dram_scratch(pref + "xv", [8192, NTOK], BF16)
    vT = p.dram_scratch(pref + "v", [1024, NTOK], BF16)
    qT = p.dram_scratch(pref + "q", [1024, NTOK], BF16)
    kT = p.dram_scratch(pref + "k", [1024, NTOK], BF16)
    ctxT = p.dram_scratch(pref + "c", [1024, NTOK], BF16)
    scT = p.dram_scratch(pref + "sc", [8192, NTOK], BF16)
    acc = [p.dram_scratch(pref + f"a{i}", [NTOK, D], F32) for i in range(2)]
    emit_ln(p, pref + "ln", x_in, xn)
    emit_mm_fm(p, pref + "lv", lvT, W["s2_sel_vT"], xn, K=1024, Mout=64)
    emit_mm_fm(p, pref + "lo", loT, W["s2_sel_oT"], xn, K=1024, Mout=64, skip_x_load=True)
    emit_gates_l0(p, pref + "g", lvT, loT, W["s2_rs"], gvT, goT)
    emit_mm_fm(p, pref + "q", qT, W["s2_q_wT"], xn, K=1024, Mout=1024)
    emit_mm_fm(p, pref + "k", kT, W["s2_k_wT"], xn, K=1024, Mout=1024, skip_x_load=True)
    for he in range(64):
        emit_mm_fm(p, pref + f"xv{he}", xvT[he*128:(he+1)*128, :], W["s2_v_w"][he], xn,
                   K=1024, Mout=128, skip_x_load=(he > 0))
    emit_gate_combine_fm(p, pref + "vc", xvT, gvT, vT, reduce_e=True)
    emit_attn(p, pref + "at", qT, kT, vT, ctxT, H=8, dq=128, dv=128,
              scale=float(1.0 / np.sqrt(128.0)), v_is_fm=True)
    emit_gate_combine_fm(p, pref + "sc", ctxT, goT, scT, reduce_e=False)
    for h in range(8):
        r_in = x_in if h == 0 else acc[(h - 1) % 2]
        r_out = x_out if h == 7 else acc[h % 2]
        emit_mm_tm(p, pref + f"o{h}", None, scT[h*1024:(h+1)*1024, :], W["s2_o_w"][h],
                   K=1024, Nout=1024, resid_dram=r_in, resid_out_f32=r_out, x_src_is_fm=True)


def emit_s3(p, pref, x_in, x_out, W):
    xn = p.dram_scratch(pref + "xn", [NTOK, D], BF16)
    q_tm = p.dram_scratch(pref + "q", [NTOK, D], BF16)
    qsq = p.dram_scratch(pref + "qs", [NTOK, 16], F32)
    av = p.dram_scratch(pref + "av", [D, D], F32)
    xav = p.dram_scratch(pref + "xav", [NTOK, D], BF16)
    qall = p.dram_scratch(pref + "qall", [8*NTOK, D], BF16)
    qsqall = p.dram_scratch(pref + "qsall", [8*NTOK, 16], F32)
    xavall = p.dram_scratch(pref + "xavall", [8*NTOK, D], BF16)
    pxav = p.dram_scratch(pref + "px", [NTOK, D], BF16)
    emit_ln(p, pref + "ln", x_in, xn)
    emit_mm_tm(p, pref + "q", q_tm, xn, W["s3_qw2d"], K=D, Nout=D)
    emit_s3_qsq(p, pref + "qs", q_tm, qsq)
    emit_s3_av(p, pref + "av", W["s3_qw2d"], W["s3_qw2dT"], W["s3_vw2d"], av)
    emit_mm_tm(p, pref + "xav", xav, xn, av, K=D, Nout=D)
    emit_s3_gather(p, pref + "g", [(q_tm, qall), (qsq, qsqall), (xav, xavall)])
    emit_s3_mix(p, pref + "mx", q_tm, qsq, qall, qsqall, xavall, pxav)
    emit_mm_tm(p, pref + "o", None, pxav, W["s3_out_wT"], K=D, Nout=D, bias_dram=W["s3_out_b"],
               resid_dram=x_in, resid_out_f32=x_out)


def emit_s5(p, pref, x_in, x_out, W):
    xn2 = p.dram_scratch(pref + "xn2", [NTOK, D], BF16)
    xn2f = p.dram_scratch(pref + "xn2f", [NTOK, D], F32)
    qT = p.dram_scratch(pref + "qT", [D, NTOK], BF16)
    kT = p.dram_scratch(pref + "kT", [D, NTOK], BF16)
    vtm = p.dram_scratch(pref + "v", [NTOK, D], BF16)
    ctxT = p.dram_scratch(pref + "c", [D, NTOK], BF16)
    emit_ln(p, pref + "ln", x_in, xn2, gamma_dram=W["s5_g"], beta_dram=W["s5_b"],
            out_tm_f32=xn2f, double_ln_eps=1e-6)
    emit_mm_fm(p, pref + "q", qT, W["s5_wq"], xn2, K=D, Mout=D)
    emit_mm_fm(p, pref + "k", kT, W["s5_wk"], xn2, K=D, Mout=D, skip_x_load=True)
    emit_mm_tm(p, pref + "v", vtm, xn2, W["s5_wv"], K=D, Nout=D)
    emit_attn(p, pref + "a", qT, kT, vtm, ctxT, H=16, dq=64, dv=64, scale=float(1.0 / 32.0))
    emit_add3(p, pref + "o", x_out, x_in, ctxT, xn2f)


def emit_s6(p, pref, x_in, x_out, W, mt6):
    xn = p.dram_scratch(pref + "xn", [NTOK, D], BF16)
    xnf = p.dram_scratch(pref + "xnf", [NTOK, D], F32)
    quT = p.dram_scratch(pref + "qu", [D, NTOK], BF16)
    qvT = p.dram_scratch(pref + "qv", [D, NTOK], BF16)
    kT = p.dram_scratch(pref + "k", [D, NTOK], BF16)
    vtm = p.dram_scratch(pref + "v", [NTOK, D], BF16)
    rT = p.dram_scratch(pref + "r", [D, S_], BF16)
    p_scr_h = p.nc.dram_tensor(pref + "P", [2, NH6, S_, S_], BF16, kind="Internal")
    p.dram[pref + "P"] = p_scr_h
    p_scr = p_scr_h.ap()
    ctxT = p.dram_scratch(pref + "c", [D, NTOK], BF16)
    s1 = p.dram_scratch(pref + "s1", [NTOK, D], F32)
    h1 = p.dram_scratch(pref + "h1", [NTOK, D], BF16)
    h1f = p.dram_scratch(pref + "h1f", [NTOK, D], F32)
    a1T = p.dram_scratch(pref + "a1", [FF6, NTOK], BF16)
    t_ = [p.dram_scratch(pref + f"t{i}", [NTOK, D], F32) for i in range(2)]
    s2 = p.dram_scratch(pref + "s2", [NTOK, D], F32)
    h2 = p.dram_scratch(pref + "h2", [NTOK, D], BF16)
    h2f = p.dram_scratch(pref + "h2f", [NTOK, D], F32)
    lgT = p.dram_scratch(pref + "lg", [16, NTOK], BF16)
    gates = p.dram_scratch(pref + "g", [NTOK, NE6], F32)
    aeT = p.dram_scratch(pref + "ae", [FF6, NTOK], BF16)
    acc = [p.dram_scratch(pref + f"acc{i}", [NTOK, D], F32) for i in range(2)]
    h3 = p.dram_scratch(pref + "h3", [NTOK, D], BF16)
    emit_ln(p, pref + "ln", x_in, xn, out_tm_f32=xnf)
    emit_mm_fm(p, pref + "qu", quT, W["s6_wq"], xn, K=D, Mout=D, bias_dram=W["s6_ub"])
    emit_mm_fm(p, pref + "qv", qvT, W["s6_wq"], xn, K=D, Mout=D, bias_dram=W["s6_vb"], skip_x_load=True)
    emit_mm_fm(p, pref + "k", kT, W["s6_wk"], xn, K=D, Mout=D, skip_x_load=True)
    emit_mm_tm(p, pref + "v", vtm, xn, W["s6_wv"], K=D, Nout=D)
    emit_mm_fm(p, pref + "r", rT, W["s6_Wr"], W["s6_peT"], K=D, Mout=D, ntok=S_,
               x_src_is_fm=True, x_cast=True)
    emit_pscore(p, pref + "p", qvT, rT, p_scr)

    def extra(b, h, qt, kc, _h=p_scr_h):
        base = (b * NH6 + h) * S_ * S_ + qt*128*S_ + (S_ - 1) - qt*128 + kc*512
        return APc(_h, base, [[S_ - 1, 128], [1, 512]])
    emit_attn(p, pref + "at", quT, kT, vtm, ctxT, H=NH6, dq=DH6, dv=DH6,
              scale=float(1.0 / np.sqrt(DH6)), causal=True, mask_tiles=mt6, extra_score=extra)
    emit_mm_tm(p, pref + "wo", None, ctxT, W["s6_Wo"], K=D, Nout=D, resid_dram=xnf,
               resid_out_f32=s1, x_src_is_fm=True)
    emit_ln(p, pref + "l1", s1, h1, gamma_dram=W["s6_l1g"], beta_dram=W["s6_l1b"], out_tm_f32=h1f)
    emit_mm_fm(p, pref + "f1", a1T, W["s6_Wff1"], h1, K=D, Mout=FF6, bias_dram=W["s6_bff1"], act=AF.Relu)
    for i in range(4):
        emit_mm_tm(p, pref + f"f2{i}", None, a1T[i*1024:(i+1)*1024, :], W["s6_Wff2"][i], K=1024, Nout=D,
                   bias_dram=W["s6_bff2"] if i == 0 else None,
                   resid_dram=h1f if i == 0 else t_[(i-1) % 2],
                   resid_out_f32=s2 if i == 3 else t_[i % 2], x_src_is_fm=True)
    emit_ln(p, pref + "l2", s2, h2, gamma_dram=W["s6_l2g"], beta_dram=W["s6_l2b"], out_tm_f32=h2f)
    emit_mm_fm(p, pref + "lg", lgT, W["s6_Wg16"], h2, K=D, Mout=16)
    emit_moe_gate(p, pref + "mg", lgT, gates)
    for e in range(NE6):
        emit_mm_fm(p, pref + f"e{e}", aeT, W["s6_We1"][e], h2, K=D, Mout=FF6,
                   bias_dram=W["s6_be1"][e], act=AF.Relu)
        for i in range(4):
            last = (i == 3)
            emit_mm_tm(p, pref + f"eo{e}_{i}", None, aeT[i*1024:(i+1)*1024, :], W["s6_We2"][e][i],
                       K=1024, Nout=D,
                       bias_dram=W["s6_be2"][e] if last else None,
                       resid_dram=None if i == 0 else t_[(i-1) % 2],
                       resid_out_f32=(acc[e % 2] if last else t_[i % 2]),
                       x_src_is_fm=True,
                       post_scale=(gates, e, NE6, (h2f if e == 0 else acc[(e-1) % 2])) if last else None)
    emit_ln(p, pref + "l3", acc[(NE6-1) % 2], h3, gamma_dram=W["s6_l3g"], beta_dram=W["s6_l3b"])
    emit_mm_tm(p, pref + "out", None, h3, W["s6_Wout"], K=D, Nout=D, resid_dram=x_in,
               resid_out_f32=x_out)


def emit_s7(p, pref, x_in, x_out, W, mask_ut, ident):
    xn = p.dram_scratch(pref + "xn", [NTOK, D], BF16)
    zT = p.dram_scratch(pref + "z", [2048, NTOK], BF16)
    xbcdtT = p.dram_scratch(pref + "xbcdt", [2336, NTOK], BF16)
    xbcT = p.dram_scratch(pref + "xbc", [2304, NTOK], BF16)
    yT = p.dram_scratch(pref + "y", [2048, NTOK], BF16)
    Pf_d = p.dram_scratch(pref + "Pf", [32, NTOK], F32)
    Pb_d = p.dram_scratch(pref + "Pb", [32, NTOK], BF16)
    dts_d = p.dram_scratch(pref + "dts", [32, NTOK], BF16)
    mo = p.dram_scratch(pref + "mo", [NTOK, 2048], BF16)
    y0T = p.dram_scratch(pref + "y0", [2048, NTOK], BF16)
    y1T = p.dram_scratch(pref + "y1", [2048, NTOK], F32)
    y1Tb = p.dram_scratch(pref + "y1b", [2048, NTOK], BF16)
    t_ = [p.dram_scratch(pref + f"t{i}", [NTOK, D], F32) for i in range(2)]
    emit_ln(p, pref + "ln", x_in, xn, gamma_dram=W["s7_norm_w"], beta_dram=None, double_ln_eps=1e-5)
    emit_mm_fm(p, pref + "z", zT, W["s7_iproj_z"], xn, K=D, Mout=2048)
    emit_mm_fm(p, pref + "xb", xbcdtT, W["s7_iproj_x"], xn, K=D, Mout=2336, skip_x_load=True)
    emit_s7_conv_dt(p, pref + "c", xbcdtT, xbcT, Pf_d, Pb_d, dts_d,
                    W["s7_conv_w"], W["s7_conv_b"], W["s7_dt_bias"], W["s7_A_log"])
    emit_s7_scan(p, pref + "s", xbcT, yT, Pf_d, Pb_d, dts_d, W["s7_D"], mask_ut, ident)
    emit_s7_gatenorm(p, pref + "g", yT, zT, mo, W["s7_gate_norm_w"])
    emit_mm_fm(p, pref + "y0", y0T, W["s7_zero_wT"], xn, K=D, Mout=2048)
    emit_mm_fm(p, pref + "y1", y1T, W["s7_one_wT"], xn, K=D, Mout=2048, skip_x_load=True, out_dtype=F32)
    emit_s7_cumsum(p, pref + "cs", y1T, y1Tb)
    for i in range(6):
        if i < 2:
            xs_ = mo[:, (i % 2)*1024:(i % 2 + 1)*1024]
            fm = False
        elif i < 4:
            xs_ = y0T[(i % 2)*1024:(i % 2 + 1)*1024, :]
            fm = True
        else:
            xs_ = y1Tb[(i % 2)*1024:(i % 2 + 1)*1024, :]
            fm = True
        emit_mm_tm(p, pref + f"f{i}", None, xs_, W["s7_fusion_wT"][i], K=1024, Nout=D,
                   resid_dram=x_in if i == 0 else t_[(i-1) % 2],
                   resid_out_f32=x_out if i == 5 else t_[i % 2], x_src_is_fm=fm)



SUBLAYERS = ["s1", "s2a", "s3", "s2b", "s5", "s6a", "s7", "s6b"]
GROUPS = [[s] for s in SUBLAYERS]


def _np_T(a):
    return np.ascontiguousarray(np.asarray(a).T)


def _host_weights(inputs):
    i = inputs
    W = {}
    W["s1_wq_dT"] = _np_T(i["L0_attn_a_wq_d"]); W["s1_bq_d"] = np.asarray(i["L0_attn_a_bq_d"])
    W["s1_wkv_dT"] = _np_T(i["L0_attn_a_wkv_d"]); W["s1_bkv_d"] = np.asarray(i["L0_attn_a_bkv_d"])
    W["s1_w_qkT"] = _np_T(i["L0_attn_a_w_qk"]); W["s1_b_qk"] = np.asarray(i["L0_attn_a_b_qk"])
    W["s1_wv_uT"] = _np_T(i["L0_attn_a_wv_u"]); W["s1_bv_u"] = np.asarray(i["L0_attn_a_bv_u"])
    W["s1_w_projT"] = _np_T(i["L0_attn_a_w_proj"]); W["s1_b_proj"] = np.asarray(i["L0_attn_a_b_proj"])
    W["s2_q_wT"] = _np_T(i["L0_ffn_q_w"]); W["s2_k_wT"] = _np_T(i["L0_ffn_k_w"])
    W["s2_v_w"] = np.ascontiguousarray(np.asarray(i["L0_ffn_v_w"]))
    W["s2_o_w"] = np.ascontiguousarray(np.asarray(i["L0_ffn_o_w"]).reshape(8192, 1024))
    W["s2_sel_vT"] = _np_T(i["L0_ffn_sel_v_w"]); W["s2_sel_oT"] = _np_T(i["L0_ffn_sel_o_w"])
    W["s2_rs"] = np.asarray(i["L0_ffn_route_scale"])
    qw2d = np.ascontiguousarray(np.asarray(i["L0_attn_b_q_weight"]).reshape(D, D))
    W["s3_qw2d"] = qw2d; W["s3_qw2dT"] = _np_T(qw2d)
    W["s3_vw2d"] = np.ascontiguousarray(np.asarray(i["L0_attn_b_v_weight"]).reshape(D, D))
    W["s3_out_wT"] = _np_T(i["L0_attn_b_out_w"]); W["s3_out_b"] = np.asarray(i["L0_attn_b_out_b"])
    W["s5_g"] = np.asarray(i["L1_attn_a_ln_scale"]); W["s5_b"] = np.asarray(i["L1_attn_a_ln_bias"])
    W["s5_wq"] = np.asarray(i["L1_attn_a_wq"]); W["s5_wk"] = np.asarray(i["L1_attn_a_wk"])
    W["s5_wv"] = np.asarray(i["L1_attn_a_wv"])
    Wqkv = np.asarray(i["L1_ffn_Wqkv"])
    W["s6_wq"] = np.ascontiguousarray(Wqkv[:, :D]); W["s6_wk"] = np.ascontiguousarray(Wqkv[:, D:2*D])
    W["s6_wv"] = np.ascontiguousarray(Wqkv[:, 2*D:])
    W["s6_ub"] = np.ascontiguousarray(np.asarray(i["L1_ffn_u_bias"]).reshape(-1))
    W["s6_vb"] = np.ascontiguousarray(np.asarray(i["L1_ffn_v_bias"]).reshape(-1))
    W["s6_Wr"] = np.asarray(i["L1_ffn_Wr"])
    pos = np.arange(S_ - 1, -1, -1, dtype=np.float32)
    inv_freq = 1.0 / 10000.0 ** (np.arange(0, D, 2, dtype=np.float32) / D)
    sinusoid = pos[:, None] * inv_freq[None, :]
    pe = np.concatenate([np.sin(sinusoid), np.cos(sinusoid)], -1).astype(np.float32)
    W["s6_peT"] = _np_T(pe)
    W["s6_Wo"] = np.asarray(i["L1_ffn_Wo"])
    W["s6_l1g"] = np.asarray(i["L1_ffn_ln1_g"]); W["s6_l1b"] = np.asarray(i["L1_ffn_ln1_b"])
    W["s6_Wff1"] = np.asarray(i["L1_ffn_Wff1"]); W["s6_bff1"] = np.asarray(i["L1_ffn_bff1"])
    Wff2 = np.asarray(i["L1_ffn_Wff2"])
    W["s6_Wff2"] = [np.ascontiguousarray(Wff2[j*1024:(j+1)*1024]) for j in range(4)]
    W["s6_bff2"] = np.asarray(i["L1_ffn_bff2"])
    W["s6_l2g"] = np.asarray(i["L1_ffn_ln2_g"]); W["s6_l2b"] = np.asarray(i["L1_ffn_ln2_b"])
    W["s6_Wg16"] = np.ascontiguousarray(np.concatenate(
        [np.asarray(i["L1_ffn_Wg"]), np.zeros((D, 8), np.float32)], 1))
    We1 = np.asarray(i["L1_ffn_We1"]); be1 = np.asarray(i["L1_ffn_be1"])
    We2 = np.asarray(i["L1_ffn_We2"]); be2 = np.asarray(i["L1_ffn_be2"])
    W["s6_We1"] = [np.ascontiguousarray(We1[e]) for e in range(8)]
    W["s6_be1"] = [np.ascontiguousarray(be1[e]) for e in range(8)]
    W["s6_We2"] = [[np.ascontiguousarray(We2[e][j*1024:(j+1)*1024]) for j in range(4)] for e in range(8)]
    W["s6_be2"] = [np.ascontiguousarray(be2[e]) for e in range(8)]
    W["s6_l3g"] = np.asarray(i["L1_ffn_ln3_g"]); W["s6_l3b"] = np.asarray(i["L1_ffn_ln3_b"])
    W["s6_Wout"] = np.asarray(i["L1_ffn_Wout"])
    W["s7_norm_w"] = np.asarray(i["L1_attn_b_norm_w"])
    ip = np.asarray(i["L1_attn_b_in_proj_w"])
    W["s7_iproj_z"] = _np_T(ip[:2048]); W["s7_iproj_x"] = _np_T(ip[2048:])
    W["s7_conv_w"] = np.asarray(i["L1_attn_b_conv_w"]); W["s7_conv_b"] = np.asarray(i["L1_attn_b_conv_b"])
    W["s7_dt_bias"] = np.asarray(i["L1_attn_b_dt_bias"]); W["s7_A_log"] = np.asarray(i["L1_attn_b_A_log"])
    W["s7_D"] = np.asarray(i["L1_attn_b_D"]); W["s7_gate_norm_w"] = np.asarray(i["L1_attn_b_gate_norm_w"])
    W["s7_zero_wT"] = _np_T(i["L1_attn_b_zero_w"]); W["s7_one_wT"] = _np_T(i["L1_attn_b_one_w"])
    fwT = _np_T(i["L1_attn_b_fusion_w"])
    W["s7_fusion_wT"] = [np.ascontiguousarray(fwT[j*1024:(j+1)*1024]) for j in range(6)]
    return W


_MASK6_NP = None
_MASKUT_NP = None


def _consts():
    global _MASK6_NP, _MASKUT_NP
    if _MASK6_NP is None:
        m = np.zeros((128, 4, 512), np.float32)
        for a in range(4):
            for r in range(128):
                m[r, a, :] = np.where(np.arange(512) <= a*128 + r, 0.0, -1e30)
        _MASK6_NP = m
        _MASKUT_NP = np.where(np.triu(np.ones((128, 128), np.float32)) > 0, 0.0, -1e30).astype(np.float32)
    return _MASK6_NP, _MASKUT_NP


_SUB_NEEDS = {
    "s1": [k for k in ["s1_wq_dT", "s1_bq_d", "s1_wkv_dT", "s1_bkv_d", "s1_w_qkT", "s1_b_qk",
                       "s1_wv_uT", "s1_bv_u", "s1_w_projT", "s1_b_proj"]],
    "s2a": ["s2_q_wT", "s2_k_wT", "s2_v_w", "s2_o_w", "s2_sel_vT", "s2_sel_oT", "s2_rs"],
    "s2b": ["s2_q_wT", "s2_k_wT", "s2_v_w", "s2_o_w", "s2_sel_vT", "s2_sel_oT", "s2_rs"],
    "s3": ["s3_qw2d", "s3_qw2dT", "s3_vw2d", "s3_out_wT", "s3_out_b"],
    "s5": ["s5_g", "s5_b", "s5_wq", "s5_wk", "s5_wv"],
    "s6a": ["s6_wq", "s6_wk", "s6_wv", "s6_ub", "s6_vb", "s6_Wr", "s6_peT", "s6_Wo", "s6_l1g", "s6_l1b",
            "s6_Wff1", "s6_bff1", "s6_Wff2", "s6_bff2", "s6_l2g", "s6_l2b", "s6_Wg16",
            "s6_We1", "s6_be1", "s6_We2", "s6_be2", "s6_l3g", "s6_l3b", "s6_Wout"],
    "s7": ["s7_norm_w", "s7_iproj_z", "s7_iproj_x", "s7_conv_w", "s7_conv_b", "s7_dt_bias",
           "s7_A_log", "s7_D", "s7_gate_norm_w", "s7_zero_wT", "s7_one_wT", "s7_fusion_wT"],
}
_SUB_NEEDS["s6b"] = _SUB_NEEDS["s6a"]


def _declare_weights(nc, Wnp, names):
    aps = {}
    flat = {}

    def decl(key, arr):
        name = "w_" + key
        t = nc.dram_tensor(name, list(arr.shape), F32, kind="ExternalInput").ap()
        flat[name] = arr
        return t
    for k in names:
        v = Wnp[k]
        if isinstance(v, list):
            if isinstance(v[0], list):
                aps[k] = [[decl(f"{k}_{a}_{b}", v[a][b]) for b in range(len(v[a]))] for a in range(len(v))]
                for a in range(len(v)):
                    for b in range(len(v[a])):
                        flat[f"w_{k}_{a}_{b}"] = v[a][b]
                continue
            aps[k] = [decl(f"{k}_{a}", v[a]) for a in range(len(v))]
            continue
        if k == "s2_v_w":
            t = nc.dram_tensor("w_" + k, list(v.shape), F32, kind="ExternalInput").ap()
            flat["w_" + k] = v
            aps[k] = [t[j] for j in range(v.shape[0])]
            continue
        if k == "s2_o_w":
            t = nc.dram_tensor("w_" + k, list(v.shape), F32, kind="ExternalInput").ap()
            flat["w_" + k] = v
            aps[k] = [t[j*1024:(j+1)*1024, :] for j in range(8)]
            continue
        aps[k] = decl(k, v)
    return aps, flat


def _emit_sub(p, sub, x_in, x_out, Waps):
    if sub == "s1":
        emit_s1(p, "S1", x_in, x_out, Waps)
    elif sub in ("s2a", "s2b"):
        emit_s2(p, sub.upper(), x_in, x_out, Waps)
    elif sub == "s3":
        emit_s3(p, "S3", x_in, x_out, Waps)
    elif sub == "s5":
        emit_s5(p, "S5", x_in, x_out, Waps)
    elif sub in ("s6a", "s6b"):
        m6np, _ = _consts()
        mt = p.sbuf("mask_tiles", [128, 4, 512], F32)
        if not getattr(p, "_m6_loaded", False):
            mtd = p.nc.inline_tensor(m6np, name="mask6_const")
            p.wait_tok("vector", p.dma("sync", mt.ap()[:], mtd.ap()))
            p._m6_loaded = True
        emit_s6(p, sub.upper(), x_in, x_out, Waps, mt.ap())
    elif sub == "s7":
        _, mutnp = _consts()
        ident = _ensure_ident(p)
        mu = p.sbuf("mask_ut", [128, 128], F32)
        if not getattr(p, "_mu_loaded", False):
            mud = p.nc.inline_tensor(mutnp, name="maskut_const")
            p.wait_tok("vector", p.dma("sync", mu.ap()[:], mud.ap()))
            p._mu_loaded = True
        emit_s7(p, "S7", x_in, x_out, Waps, mu, ident)
    else:
        raise ValueError(sub)


def kernel(**inputs):
    x_full = np.ascontiguousarray(np.asarray(inputs["L0_attn_a_x"], dtype=np.float32))
    Wnp = _host_weights(inputs)
    shards = [np.ascontiguousarray(x_full[2*c:2*c+2].reshape(NTOK, D)) for c in range(8)]
    for gi, group in enumerate(GROUPS):
        nc = bass.Bass("TRN2", target_bir_lowering=False, disable_frame_to_traceback=True)
        need = []
        for s in group:
            need += _SUB_NEEDS[s]
        need = list(dict.fromkeys(need))
        x_in = nc.dram_tensor("x", [NTOK, D], F32, kind="ExternalInput").ap()
        x_out = nc.dram_tensor("xout", [NTOK, D], F32, kind="ExternalOutput").ap()
        Waps, flat = _declare_weights(nc, Wnp, need)
        p = Prog(nc)
        cur_in = x_in
        for si, s in enumerate(group):
            last = (si == len(group) - 1)
            nxt = x_out if last else p.dram_scratch(f"xchain{si}", [NTOK, D], F32)
            _emit_sub(p, s, cur_in, nxt, Waps)
            cur_in = nxt
        p.build()
        import sys as _sys, time as _time
        print(f"[kernel] launch {gi} ({group}): {nc.n_instructions()} instrs, building done {_time.strftime('%H:%M:%S')}", flush=True)
        in_maps = [dict(flat, x=shards[c]) for c in range(8)]
        res = run_bass_kernel_spmd(nc, in_maps, core_ids=list(range(8)))
        print(f"[kernel] launch {gi} done {_time.strftime('%H:%M:%S')}", flush=True)
        shards = [np.asarray(res.results[c]["xout"]) for c in range(8)]
    out = np.concatenate([s.reshape(2, T, D) for s in shards], 0)
    return out
